# version 01 (final)
# baseline (speedup 1.0000x reference)
import numpy as np
import concourse.bass as bass
import concourse.mybir as mybir
from concourse.bass_utils import run_bass_kernel_spmd

F32 = mybir.dt.float32
F32R = mybir.dt.float32r
BF16 = mybir.dt.bfloat16
I32 = mybir.dt.int32
AF = mybir.ActivationFunctionType
ALU = mybir.AluOpType

B = 4
S = 2048
D = 2048
II = 8192
NKC = 16
NIC = 64
TOK = 1024
DH = 128
KV0 = 4
EPS_LN = 1e-5
EPS_RMS = 1e-8
EPS_TQ = 1e-8
ATANH_HALF = 0.5493061443340549
SM_SHIFT = 24.0
ISQ_DH = float(1.0 / np.sqrt(128.0))
QT = 512
T0 = 512


ENGINES = ("sync", "scalar", "vector", "tensor", "gpsimd")

A1_CSQ = 0
A1_CSK = 2048
A1_CSV = 2560
NAR1 = 3072
A2_CSG = 0
A2_CSU = 8192
A2_CSQ1 = 16384
A2_CSK1 = 18432
A2_CSV1 = 20480
A2_AW = 22528
NAR2 = 22532


def dt_size(dt):
    return np.dtype(mybir.dt.np(dt)).itemsize


class Ctr:
    def __init__(self, nc, name):
        self.h = nc.alloc_semaphore(name)
        self.n = 0

    @property
    def dep(self):
        return (self, self.n)


class Kern:
    def __init__(self, upto=99):
        self.upto = upto
        nc = bass.Bass("TRN2", target_bir_lowering=False)
        self.nc = nc
        self.q = {e: [] for e in ENGINES}
        self.sems = {}
        self.names = 0
        self.dbg = {}
        self._declare_io()
        self._declare_scratch()
        self._sbuf_map()
        self.ps = [nc.alloc_psum_tensor(f"pb{i}", [128, 512], F32) for i in range(8)]
        self.ps_rel = [[] for _ in range(8)]
        self.mm = self.sem("c_mm")
        self.ve = self.sem("c_ve")
        self.ac = self.sem("c_ac")
        self.gp = self.sem("c_gp")
        self.cc = self.sem("c_cc")
        self.dq = {"sync": self.sem("d_sy0"), "scalar": self.sem("d_sc0"),
                   "gpsimd": self.sem("d_gp0")}
        self.ectr = {"tensor": self.mm, "vector": self.ve, "scalar": self.ac,
                     "gpsimd": self.gp}

    def sem(self, name):
        if name not in self.sems:
            self.sems[name] = Ctr(self.nc, name)
        return self.sems[name]

    def _nm(self, p):
        self.names += 1
        return f"{p}_{self.names}"

    def E(self, eng, op, waits=(), inc=True):
        ws = [(cv[0].h, int(cv[1])) for cv in waits if cv[1] > 0]
        pair = None
        if inc is True and eng in self.ectr:
            ctr = self.ectr[eng]
            ctr.n += 1
            pair = (ctr.h, 1)
        elif isinstance(inc, tuple):
            ctr, k = inc
            ctr.n += k
            pair = (ctr.h, k)

        def fn(e, op=op, ws=ws, pair=pair):
            for h, v in ws:
                e.wait_ge(h, v)
            r = op(e)
            if pair is not None:
                r.then_inc(pair[0], pair[1])
        self.q[eng].append(fn)

    def dma(self, qeng, out, in_, waits=(), sem_override=None, **kw):
        ctr = sem_override if sem_override is not None else self.dq[qeng]
        self.E(qeng, lambda e: e.dma_start(out=out, in_=in_, **kw),
               waits=waits, inc=(ctr, 16))
        return (ctr, ctr.n)

    def dump(self, name, ap_fn, shape, dt=F32, qeng="sync", waits=()):
        t = self.nc.dram_tensor(f"dbg_{name}", list(shape), dt,
                                kind="ExternalOutput").ap()
        self.dbg[name] = t
        return self.dma(qeng, t, ap_fn, waits=waits)

    def _declare_io(self):
        nc = self.nc
        A = {}

        def di(name, shape, dt=F32):
            A[name] = nc.dram_tensor(name, list(shape), dt, kind="ExternalInput").ap()
        di("x", (D, TOK))
        di("wq_s", (256, D)); di("wk_s", (256, 512)); di("wv_s", (256, 512))
        di("wo_s", (256, D))
        di("bq", (128, NKC)); di("bk", (128, KV0)); di("bv", (128, KV0)); di("bo", (128, NKC))
        di("bvr", (1, 512))
        di("gwt_s", (256, II), I32); di("uwt_s", (256, II), I32); di("dwt_s", (1024, D), I32)
        di("gs", (128, NIC)); di("us", (128, NIC)); di("ds", (128, NKC))
        di("wq1_s", (16, 256, 128)); di("wk1_s", (16, 256, 128)); di("wv1_s", (16, 256, 128))
        di("wo1_s", (256, D))
        di("w1t_s", (256, II)); di("w2t_s", (256, II)); di("w3t_s", (1024, D))
        di("g1", (128, NKC)); di("g2", (128, NKC)); di("g3", (128, NIC))
        di("cmask", (2, NKC, 128, QT), BF16)
        di("c_ones_b", (128, 1), BF16)
        di("c_ones_f", (128, 1))
        di("c_onesrow_b", (1, 128), BF16)
        self.inp = A
        self.out = nc.dram_tensor("out", [D, TOK], F32, kind="ExternalOutput").ap()

    def _declare_scratch(self):
        nc = self.nc
        Dm = {}

        def dscr(name, shape, dt=F32):
            Dm[name] = nc.dram_tensor(name, list(shape), dt).ap()
        dscr("cci_g", (256, II), BF16); dscr("cco_g", (D, II), BF16)
        dscr("cci_u", (256, II), BF16); dscr("cco_u", (D, II), BF16)
        dscr("cci_d", (1024, D), BF16); dscr("cco_d", (II, D), BF16)
        dscr("cci_m1", (256, II), BF16); dscr("cco_m1", (D, II), BF16)
        dscr("cci_m2", (256, II), BF16); dscr("cco_m2", (D, II), BF16)
        dscr("cci_m3", (1024, D), BF16); dscr("cco_m3", (II, D), BF16)
        dscr("cci_r1", (1, NAR1)); dscr("cco_r1", (1, NAR1))
        dscr("cci_r2", (1, NAR2)); dscr("cco_r2", (1, NAR2))
        dscr("cci_pq", (2, 256, D)); dscr("cco_pq", (8, 2, 256, D))
        dscr("cci_pk", (2, 256, 512)); dscr("cco_pk", (8, 2, 256, 512))
        dscr("cci_pv", (2, 256, 512)); dscr("cco_pv", (8, 2, 256, 512))
        dscr("cci_po", (2, 256, D)); dscr("cco_po", (8, 2, 256, D))
        dscr("cci_pq1", (2, 16, 256, 128)); dscr("cco_pq1", (8, 2, 16, 256, 128))
        dscr("cci_pk1", (2, 16, 256, 128)); dscr("cco_pk1", (8, 2, 16, 256, 128))
        dscr("cci_pv1", (2, 16, 256, 128)); dscr("cco_pv1", (8, 2, 16, 256, 128))
        dscr("cci_po1", (2, 256, D)); dscr("cco_po1", (8, 2, 256, D))
        dscr("cci_kv0", (1024 + 1024, 1024))
        dscr("cco_kv0", (2, 2048, 1024))
        dscr("cci_kv1", (D, D))
        dscr("cco_kv1", (2, D, D))
        dscr("q0h", (D, TOK)); dscr("q0l", (D, TOK))
        dscr("a0h", (D, TOK)); dscr("a0l", (D, TOK))
        dscr("q1h", (D, TOK)); dscr("q1l", (D, TOK))
        dscr("a1h", (D, TOK)); dscr("a1l", (D, TOK))
        dscr("hlo0", (II, T0), BF16)
        dscr("thr_d", (1, 16))
        dscr("row_d", (1, 1024))
        dscr("hbuf1", (II, TOK))
        self.scr = Dm

    def _sbuf_map(self):
        nc = self.nc
        BASE = 16512
        self.OFF_RES = BASE
        self.OFF_A = BASE + 65536
        self.OFF_B = self.OFF_A + 65536
        self.OFF_W = self.OFF_B + 32768
        self.OFF_BC = self.OFF_W + 16384
        self.OFF_ROWS = self.OFF_BC + 8192
        self.OFF_SCL = self.OFF_ROWS + 19456
        self.OFF_CST = self.OFF_SCL + 2048
        self.END = self.OFF_CST + 1024
        assert self.END <= 229376, self.END

        self.res = [nc.alloc_sbuf_tensor_at(f"res{k}", [128, TOK], F32,
                                            offset=self.OFF_RES + k * 4096)
                    for k in range(NKC)]
        self.bc = [nc.alloc_sbuf_tensor_at(f"bct{k}", [128, TOK], F32,
                                           offset=self.OFF_BC + k * 4096)
                   for k in range(2)]
        self._row_off = self.OFF_ROWS
        self.rows = {}
        scl_off = self.OFF_SCL
        self.scl = {}
        for nm, cols in (("gs", NIC), ("us", NIC), ("ds", NKC), ("g1", NKC),
                         ("g2", NKC), ("g3", NIC), ("bq", NKC), ("bk", KV0),
                         ("bv", KV0), ("bo", NKC)):
            self.scl[nm] = nc.alloc_sbuf_tensor_at(
                f"scl_{nm}", [128, cols], F32, offset=scl_off)
            scl_off += ((cols * 4) + 31) & ~31
        assert scl_off <= self.OFF_CST, scl_off
        self.ones_r = nc.alloc_sbuf_tensor_at("ones_r", [128, 1], F32R,
                                              offset=self.OFF_CST)
        self.ones_b = nc.alloc_sbuf_tensor_at("ones_b", [128, 1], BF16,
                                              offset=self.OFF_CST + 32)
        self.ones_f = nc.alloc_sbuf_tensor_at("ones_f", [128, 1], F32,
                                              offset=self.OFF_CST + 64)
        self.ones_row_r = nc.alloc_sbuf_tensor_at("ones_row_r", [1, 128], F32R,
                                                  offset=self.OFF_CST + 128)
        self.onesrow_b = nc.alloc_sbuf_tensor_at("onesrow_b", [1, 128], BF16,
                                                 offset=self.OFF_CST + 672)

    def phase_gate(self):
        return [(c, c.n) for c in self.sems.values() if c.n > 0]

    def psw(self, i):
        return list(self.ps_rel[i])

    def ps_done(self, i, dep):
        self.ps_rel[i] = [dep]

    def row(self, name, width, dt=F32):
        key = name
        if key not in self.rows:
            t = self.nc.alloc_sbuf_tensor_at(f"row_{name}", [1, width], dt,
                                             offset=self._row_off)
            self._row_off += ((width * dt_size(dt)) + 31) & ~31
            assert self._row_off <= self.OFF_SCL, "rows overflow"
            self.rows[key] = t
        return self.rows[key]

    def vA(self, shape, dt, off_bytes, name=None):
        return self.nc.alloc_sbuf_tensor_at(name or self._nm("vA"), list(shape), dt,
                                            offset=self.OFF_A + off_bytes)

    def vB(self, shape, dt, off_bytes, name=None):
        return self.nc.alloc_sbuf_tensor_at(name or self._nm("vB"), list(shape), dt,
                                            offset=self.OFF_B + off_bytes)

    def vW(self, shape, dt, off_bytes, name=None):
        return self.nc.alloc_sbuf_tensor_at(name or self._nm("vW"), list(shape), dt,
                                            offset=self.OFF_W + off_bytes)



def host_prepare(inputs):
    f32 = np.float32
    x = np.asarray(inputs["L0_attn_x"])
    common = {}
    wq = np.ascontiguousarray(inputs["L0_attn_Wq"], f32)
    wk = np.ascontiguousarray(inputs["L0_attn_Wk"], f32)
    wv = np.ascontiguousarray(inputs["L0_attn_Wv"], f32)
    wo = np.ascontiguousarray(inputs["L0_attn_Wo"], f32)
    common["bq"] = np.asarray(inputs["L0_attn_bq"], f32).reshape(16, 128)
    common["bk"] = np.asarray(inputs["L0_attn_bk"], f32).reshape(4, 128)
    common["bv"] = np.asarray(inputs["L0_attn_bv"], f32).reshape(4, 128)
    common["bo"] = np.asarray(inputs["L0_attn_bo"], f32).reshape(16, 128)
    gwt = np.ascontiguousarray(np.asarray(inputs["L0_ffn_gate_w"]).T)
    uwt = np.ascontiguousarray(np.asarray(inputs["L0_ffn_up_w"]).T)
    dwt = np.ascontiguousarray(np.asarray(inputs["L0_ffn_down_w"]).T)
    common["gs"] = np.asarray(inputs["L0_ffn_gate_scale"], f32).reshape(64, 128)
    common["us"] = np.asarray(inputs["L0_ffn_up_scale"], f32).reshape(64, 128)
    common["ds"] = np.asarray(inputs["L0_ffn_down_scale"], f32).reshape(16, 128)
    wq1 = np.ascontiguousarray(np.asarray(inputs["L1_attn_W_Q"], f32).transpose(0, 2, 1))
    wk1 = np.ascontiguousarray(np.asarray(inputs["L1_attn_W_K"], f32).transpose(0, 2, 1))
    wv1 = np.ascontiguousarray(np.asarray(inputs["L1_attn_W_V"], f32).transpose(0, 2, 1))
    wo1 = np.ascontiguousarray(
        np.asarray(inputs["L1_attn_W_O"], f32).transpose(0, 2, 1).reshape(2048, 2048))
    common["bvr"] = np.ascontiguousarray(np.asarray(inputs["L0_attn_bv"], f32).reshape(1, 512))
    w1t = np.ascontiguousarray(np.asarray(inputs["L1_ffn_w1"], f32).T)
    w2t = np.ascontiguousarray(np.asarray(inputs["L1_ffn_w2"], f32).T)
    w3t = np.ascontiguousarray(np.asarray(inputs["L1_ffn_w3"], f32).T)
    common["g1"] = np.asarray(inputs["L1_ffn_g1"], f32).reshape(16, 128)
    common["g2"] = np.asarray(inputs["L1_ffn_g2"], f32).reshape(16, 128)
    common["g3"] = np.asarray(inputs["L1_ffn_g3"], f32).reshape(64, 128)
    for nm in ("gs", "us", "ds", "g1", "g2", "g3", "bq", "bk", "bv", "bo"):
        common[nm] = np.ascontiguousarray(common[nm].T)

    import ml_dtypes
    common["c_ones_b"] = np.ones((128, 1), ml_dtypes.bfloat16)
    common["c_ones_f"] = np.ones((128, 1), f32)
    common["c_onesrow_b"] = np.ones((1, 128), ml_dtypes.bfloat16)

    per_core = []
    for c in range(8):
        b, half = c // 2, c % 2
        d = {}
        d["x"] = np.ascontiguousarray(x[b].T[:, half * TOK:(half + 1) * TOK], f32)
        import ml_dtypes
        cm = np.zeros((2, NKC, 128, QT), np.float32)
        for qt_ in range(2):
            qg = half * 1024 + qt_ * 512 + np.arange(QT)[None, :]
            for sc_ in range(NKC):
                sg = sc_ * 128 + np.arange(128)[:, None]
                cm[qt_, sc_] = np.where(qg >= sg, 0.0, -1e9)
        d["cmask"] = np.ascontiguousarray(cm.astype(ml_dtypes.bfloat16))
        ksl = slice(c * 256, (c + 1) * 256)
        isl = slice(c * 1024, (c + 1) * 1024)
        d["wq_s"] = np.ascontiguousarray(wq[ksl])
        d["wk_s"] = np.ascontiguousarray(wk[ksl])
        d["wv_s"] = np.ascontiguousarray(wv[ksl])
        d["wo_s"] = np.ascontiguousarray(wo[ksl])
        d["wq1_s"] = np.ascontiguousarray(wq1[:, ksl, :])
        d["wk1_s"] = np.ascontiguousarray(wk1[:, ksl, :])
        d["wv1_s"] = np.ascontiguousarray(wv1[:, ksl, :])
        d["wo1_s"] = np.ascontiguousarray(wo1[ksl])
        d["gwt_s"] = np.ascontiguousarray(gwt[ksl])
        d["uwt_s"] = np.ascontiguousarray(uwt[ksl])
        d["dwt_s"] = np.ascontiguousarray(dwt[isl])
        d["w1t_s"] = np.ascontiguousarray(w1t[ksl], f32)
        d["w2t_s"] = np.ascontiguousarray(w2t[ksl], f32)
        d["w3t_s"] = np.ascontiguousarray(w3t[isl], f32)
        per_core.append(d)
    return common, per_core



def ph_init(K):
    ld_x = K.sem("ld_x")
    ld_scl = K.sem("ld_scl")
    for k in range(NKC):
        K.dma("sync", K.res[k][:, :], K.inp["x"][k * 128:(k + 1) * 128, :],
              sem_override=ld_x)
    for nm in ("gs", "us", "ds", "g1", "g2", "g3", "bq", "bk", "bv", "bo"):
        K.dma("scalar", K.scl[nm][:, :], K.inp[nm], sem_override=ld_scl)
    ld_c0 = K.sem("ld_c")
    for t, nm in ((K.ones_b, "c_ones_b"), (K.ones_f, "c_ones_f"),
                  (K.onesrow_b, "c_onesrow_b")):
        K.dma("scalar", t[:, :], K.inp[nm], sem_override=ld_c0)
    ld_c = K.sem("ld_c")
    for t, nm2 in ((K.ones_b, None),):
        pass
    dep_c = (ld_c, ld_c.n)
    K.E("vector", lambda e: e.tensor_copy(K.ones_r[:, :], K.ones_f[:, :]),
        waits=[dep_c])
    K.E("vector", lambda e: e.tensor_copy(K.ones_row_r[0:1, :],
                                          K.onesrow_b[0:1, :]),
        waits=[K.ve.dep])
    K.dep_x = (ld_x, ld_x.n)
    K.dep_scl = (ld_scl, ld_scl.n)
    K.dep_ones = K.ve.dep
    zt = K.vB((128, 1024), F32, 0, "init_zero")
    K.E("vector", lambda e: e.memset(zt[:, :], 0.0))
    dep_z = K.ve.dep
    zsem = K.sem("ld_zero")
    for r in range(16):
        K.dma("sync", K.scr["cci_kv0"][r * 128:(r + 1) * 128, :], zt[:, :],
              waits=[dep_z], sem_override=zsem)
    for r in range(16):
        K.dma("sync", K.scr["cci_kv1"][r * 128:(r + 1) * 128, 0:1024], zt[:, :],
              waits=[dep_z], sem_override=zsem)
        K.dma("sync", K.scr["cci_kv1"][r * 128:(r + 1) * 128, 1024:2048],
              zt[:, :], waits=[dep_z], sem_override=zsem)
    K.dep_zero = (zsem, zsem.n)



class Ring:

    def __init__(self, K, name, tiles, qeng):
        self.K = K
        self.tiles = tiles
        self.q = qeng
        self.sems = [K.sem(f"rg_{name}{i}") for i in range(len(tiles))]
        self.rel = [[] for _ in tiles]
        self.i = 0

    def fill(self, src_ap, tile_slice=None, extra_waits=(), **kw):
        sl = self.i % len(self.tiles)
        self.i += 1
        sem = self.sems[sl]
        w = list(self.rel[sl]) + list(extra_waits)
        t = self.tiles[sl]
        dst = t[:, :] if tile_slice is None else tile_slice(t)
        self.K.E(self.q, lambda e, dst=dst, src=src_ap, kw=kw:
                 e.dma_start(out=dst, in_=src, **kw), waits=w, inc=(sem, 16))
        return sl, t, (sem, sem.n)

    def release(self, sl, *deps):
        self.rel[sl] = list(deps)


class StoreGate:

    def __init__(self, K, name, n, qeng):
        self.K = K
        self.q = qeng
        self.sems = [K.sem(f"sg_{name}{i}") for i in range(n)]

    def store(self, sl, dst_ap, src_ap, waits=()):
        sem = self.sems[sl]
        self.K.E(self.q, lambda e, d=dst_ap, s=src_ap: e.dma_start(out=d, in_=s),
                 waits=waits, inc=(sem, 16))
        return (sem, sem.n)





def ag(K, nin, nout, waits, pairs=False):
    rg = [[0, 1], [2, 3], [4, 5], [6, 7]] if pairs else [list(range(8))]
    K.E("gpsimd", lambda e, a=K.scr[nin], b=K.scr[nout], rg=rg:
        e.collective_compute("AllGather", ALU.bypass, replica_groups=rg,
                             ins=[a], outs=[b]),
        waits=waits, inc=(K.cc, 1))
    return (K.cc, K.cc.n)



def prep_alloc(K):
    if hasattr(K, "prep_tf"):
        return
    K.prep_tf = [K.vA((128, 2048), F32, 0, "pr_tf0"),
                 K.vA((128, 2048), F32, 8192, "pr_tf1")]
    K.prep_hi = [K.vA((128, 2048), F32R, 16384, "pr_hi0"),
                 K.vA((128, 2048), F32R, 24576, "pr_hi1")]
    K.prep_lo = [K.vA((128, 2048), F32R, 32768, "pr_lo0"),
                 K.vA((128, 2048), F32R, 40960, "pr_lo1")]
    K.r_ptf = Ring(K, "p_tf", K.prep_tf, "sync")
    K.g_pw = StoreGate(K, "p_pw", 2, "scalar")
    K.g_row = StoreGate(K, "p_row", 1, "scalar")
    K.prep_evr = K.row("evr0", 512)
    K._evr_rel = None


def evr_store(K, dst_ap, width=512, scale=None):
    evr = K.prep_evr
    w = [K.mm.dep] + ([K._evr_rel] if K._evr_rel else [])
    if scale is None:
        K.E("vector", lambda e: e.tensor_copy(evr[0:1, 0:width],
                                              K.ps[6][0:1, 0:width]), waits=w)
    else:
        K.E("vector", lambda e: e.tensor_scalar(evr[0:1, 0:width],
                                                K.ps[6][0:1, 0:width],
                                                scale, None, ALU.mult), waits=w)
    K.ps_done(6, K.ve.dep)
    K._evr_rel = K.g_row.store(0, dst_ap, evr[0:1, 0:width], waits=[K.ve.dep])


def split_shard(K, src_rows, cci_pair, ncols, cs_dst=None):
    for c0 in range(0, ncols, 2048):
        wdt = min(2048, ncols - c0)
        info = []
        for kk in range(2):
            sl, t_in, dep_in = K.r_ptf.fill(
                src_rows(kk, c0, wdt), tile_slice=lambda t, wdt=wdt: t[:, 0:wdt])
            K.E("vector", lambda e, sl=sl, wdt=wdt:
                e.tensor_copy(K.prep_hi[sl][:, 0:wdt], K.prep_tf[sl][:, 0:wdt]),
                waits=[dep_in] + K._psplit_rel[sl] + [K.ve.dep])
            K.E("vector", lambda e, sl=sl, wdt=wdt:
                e.tensor_sub(K.prep_lo[sl][:, 0:wdt], K.prep_tf[sl][:, 0:wdt],
                             K.prep_hi[sl].ap().bitcast(F32)[:, 0:wdt]),
                waits=[K.ve.dep])
            dep_sp = K.ve.dep
            d1 = K.g_pw.store(sl, cci_pair(0, kk, c0, wdt),
                              K.prep_hi[sl].ap().bitcast(F32)[:, 0:wdt],
                              waits=[dep_sp])
            d2 = K.g_pw.store(sl, cci_pair(1, kk, c0, wdt),
                              K.prep_lo[sl].ap().bitcast(F32)[:, 0:wdt],
                              waits=[dep_sp])
            K._psplit_rel[sl] = [d2]
            info.append((sl, dep_in, dep_sp))
        if cs_dst is not None:
            for j0 in range(0, wdt, 512):
                jw = min(512, wdt - j0)
                for kk, (sl, dep_in, dep_sp) in enumerate(info):
                    K.E("tensor", lambda e, sl=sl, j0=j0, jw=jw, kk=kk:
                        e.matmul(K.ps[6][0:1, 0:jw], K.ones_f[:, :],
                                 K.prep_tf[sl][:, j0:j0 + jw],
                                 start=(kk == 0), stop=(kk == 1)),
                        waits=[dep_in] + (K.psw(6) if kk == 0 else []))
                evr_store(K, cs_dst(c0 + j0, jw), jw)
        for kk, (sl, dep_in, dep_sp) in enumerate(info):
            K.r_ptf.release(sl, dep_sp, K.mm.dep)


def ph_prep_l0attn_w(K):
    prep_alloc(K)
    K._psplit_rel = [[], []]
    cs_psum_note = None
    for nm, ncols, aroff in (("wq", D, A1_CSQ), ("wk", 512, A1_CSK),
                             ("wv", 512, A1_CSV)):
        src = K.inp[nm + "_s"]
        cci = K.scr["cci_p" + nm[1]]
        split_shard(
            K,
            lambda kk, c0, wdt, src=src: src[kk * 128:(kk + 1) * 128,
                                             c0:c0 + wdt],
            lambda pl, kk, c0, wdt, cci=cci: cci[pl, kk * 128:(kk + 1) * 128,
                                                 c0:c0 + wdt],
            ncols,
            cs_dst=lambda c0, jw, aroff=aroff: K.scr["cci_r1"][0:1,
                                                              aroff + c0:
                                                              aroff + c0 + jw])
    dep_rows = (K.g_row.sems[0], K.g_row.sems[0].n)
    dep_pw = [(sm, sm.n) for sm in K.g_pw.sems]
    K.E("gpsimd", lambda e: e.collective_compute(
        "AllReduce", ALU.add, replica_groups=[list(range(8))],
        ins=[K.scr["cci_r1"]], outs=[K.scr["cco_r1"]]),
        waits=[dep_rows], inc=(K.cc, 1))
    K.cc_ar1 = K.cc.n
    K.cc_pq = ag(K, "cci_pq", "cco_pq", dep_pw)
    K.cc_pk = ag(K, "cci_pk", "cco_pk", dep_pw)
    K.cc_pv = ag(K, "cci_pv", "cco_pv", dep_pw)


def ph_prep_wo(K):
    K.r_ptf.rel = [K.phase_gate(), K.phase_gate()]
    K._psplit_rel = [K.phase_gate(), K.phase_gate()]
    split_shard(
        K,
        lambda kk, c0, wdt: K.inp["wo_s"][kk * 128:(kk + 1) * 128, c0:c0 + wdt],
        lambda pl, kk, c0, wdt: K.scr["cci_po"][pl, kk * 128:(kk + 1) * 128,
                                                c0:c0 + wdt],
        D, cs_dst=None)
    dep_pw = [(sm, sm.n) for sm in K.g_pw.sems]
    K.cc_po = ag(K, "cci_po", "cco_po", dep_pw)


def ph_prep_l1attn_w(K):
    K.r_ptf.rel = [K.phase_gate(), K.phase_gate()]
    K._psplit_rel = [K.phase_gate(), K.phase_gate()]
    for nm, cci_nm, aroff in (("wq1", "cci_pq1", A2_CSQ1),
                              ("wk1", "cci_pk1", A2_CSK1),
                              ("wv1", "cci_pv1", A2_CSV1)):
        src = K.inp[nm + "_s"]
        cci = K.scr[cci_nm]
        for h in range(16):
            split_shard(
                K,
                lambda kk, c0, wdt, src=src, h=h:
                    src[h, kk * 128:(kk + 1) * 128, :],
                lambda pl, kk, c0, wdt, cci=cci, h=h:
                    cci[pl, h, kk * 128:(kk + 1) * 128, :],
                128,
                cs_dst=lambda c0, jw, aroff=aroff, h=h:
                    K.scr["cci_r2"][0:1, aroff + h * 128 + c0:
                                    aroff + h * 128 + c0 + jw])
    split_shard(
        K,
        lambda kk, c0, wdt: K.inp["wo1_s"][kk * 128:(kk + 1) * 128, c0:c0 + wdt],
        lambda pl, kk, c0, wdt: K.scr["cci_po1"][pl, kk * 128:(kk + 1) * 128,
                                                 c0:c0 + wdt],
        D, cs_dst=None)
    dep_rows = [(K.g_row.sems[0], K.g_row.sems[0].n)]
    dep_pw = [(sm, sm.n) for sm in K.g_pw.sems]
    K.E("gpsimd", lambda e: e.collective_compute(
        "AllReduce", ALU.add, replica_groups=[list(range(8))],
        ins=[K.scr["cci_r2"]], outs=[K.scr["cco_r2"]]),
        waits=dep_rows, inc=(K.cc, 1))
    K.cc_ar2 = K.cc.n
    K.cc_pq1 = ag(K, "cci_pq1", "cco_pq1", dep_pw)
    K.cc_pk1 = ag(K, "cci_pk1", "cco_pk1", dep_pw)
    K.cc_pv1 = ag(K, "cci_pv1", "cco_pv1", dep_pw)
    K.cc_po1 = ag(K, "cci_po1", "cco_po1", dep_pw)


def ph_prep_gud(K):
    ti = [K.vA((128, 2048), BF16, 49152, "pr_ti0"),
          K.vA((128, 2048), BF16, 53248, "pr_ti1")]
    tw = [K.vA((128, 2048), BF16, 57344, "pr_tw0"),
          K.vA((128, 2048), BF16, 61440, "pr_tw1")]
    r_ti = Ring(K, "p_ti", ti, "gpsimd")
    r_ti.rel = [K.phase_gate(), K.phase_gate()]
    K.r_ptf.rel = [K.phase_gate(), K.phase_gate()]
    g_int = StoreGate(K, "p_int", 2, "scalar")
    K.g_int = g_int
    tw_rel = [[], []]

    def conv_slice(src, cci, r0, c0):
        sl, t_in, dep_in = r_ti.fill(src[r0:r0 + 128, c0:c0 + 2048])
        K.E("vector", lambda e, sl=sl, b=t_in:
            e.tensor_scalar(tw[sl][:, :], b[:, :], -128.0, None, ALU.add),
            waits=[dep_in] + tw_rel[sl])
        dep_w = K.ve.dep
        r_ti.release(sl, dep_w)
        dep_st = g_int.store(sl, cci[r0:r0 + 128, c0:c0 + 2048], tw[sl][:, :],
                             waits=[dep_w])
        tw_rel[sl] = [dep_st]
        return sl, dep_w

    for mat, src, cci, aroff in (("g", K.inp["gwt_s"], K.scr["cci_g"], A2_CSG),
                                 ("u", K.inp["uwt_s"], K.scr["cci_u"], A2_CSU)):
        for cs_i in range(4):
            c0 = cs_i * 2048
            deps = [conv_slice(src, cci, kk * 128, c0) for kk in range(2)]
            for j in range(4):
                for kk, (sl, dep_w) in enumerate(deps):
                    K.E("tensor", lambda e, sl=sl, j=j, kk=kk:
                        e.matmul(K.ps[6][0:1, 0:512], K.ones_b[:, :],
                                 tw[sl][:, j * 512:(j + 1) * 512],
                                 start=(kk == 0), stop=(kk == 1)),
                        waits=[dep_w] + (K.psw(6) if kk == 0 else []))
                evr_store(K, K.scr["cci_r2"][0:1, aroff + c0 + j * 512:
                                             aroff + c0 + (j + 1) * 512])
            for kk, (sl, _) in enumerate(deps):
                tw_rel[sl] = tw_rel[sl] + [K.mm.dep]
    for kk in range(8):
        conv_slice(K.inp["dwt_s"], K.scr["cci_d"], kk * 128, 0)

    aw_acc = K.vW((128, 4), F32, 12288, "pr_awacc")
    aw_tmp = K.vW((128, 2), F32, 12320, "pr_awtmp")
    aw_row = K.row("awrow", 4)
    K.E("vector", lambda e: e.memset(aw_acc[:, :], 0.0), waits=K.phase_gate())
    for wi_, (src, nrow, ncol) in enumerate(((K.inp["w1t_s"], 256, II),
                                             (K.inp["w2t_s"], 256, II),
                                             (K.inp["w3t_s"], 1024, D))):
        for r0 in range(0, nrow, 128):
            for c0 in range(0, ncol, 2048):
                sl, t_in, dep_in = K.r_ptf.fill(src[r0:r0 + 128, c0:c0 + 2048])
                K.E("vector", lambda e, t=t_in:
                    e.reduce_sum(aw_tmp[:, 0:1], t[:, :],
                                 axis=mybir.AxisListType.X,
                                 apply_absolute_value=True),
                    waits=[dep_in, K.ve.dep])
                K.r_ptf.release(sl, K.ve.dep)
                K.E("vector", lambda e, wi_=wi_:
                    e.tensor_add(aw_acc[:, wi_:wi_ + 1], aw_acc[:, wi_:wi_ + 1],
                                 aw_tmp[:, 0:1]), waits=[K.ve.dep])
    K.E("tensor", lambda e: e.matmul(K.ps[6][0:1, 0:4], K.ones_f[:, :],
                                     aw_acc[:, 0:4], start=True, stop=True),
        waits=[K.ve.dep] + K.psw(6))
    K.E("vector", lambda e: e.tensor_copy(aw_row[0:1, :], K.ps[6][0:1, 0:4]),
        waits=[K.mm.dep, K._evr_rel if K._evr_rel else (K.mm, 0)])
    K.ps_done(6, K.ve.dep)
    K._evr_rel = K.g_row.store(0, K.scr["cci_r2"][0:1, A2_AW:A2_AW + 4],
                               aw_row[0:1, :], waits=[K.ve.dep])
    K.cc_gate = ag(K, "cci_g", "cco_g", [(s, s.n) for s in g_int.sems])
    K.cc_up = ag(K, "cci_u", "cco_u", [(s, s.n) for s in g_int.sems])
    K.cc_down = ag(K, "cci_d", "cco_d", [(s, s.n) for s in g_int.sems])


def ph_prep_tern(K):
    thr = K.row("thr", 16)
    tm = [K.vA((128, 2048), BF16, 49152, "pr_tm0"),
          K.vA((128, 2048), BF16, 53248, "pr_tm1")]
    tsg = [K.vA((128, 2048), BF16, 57344, "pr_tsg0"),
           K.vA((128, 2048), BF16, 61440, "pr_tsg1")]
    tcol = [K.vW((128, 1), F32, 0, "pr_tc0"),
            K.vW((128, 1), F32, 32, "pr_tc1")]
    g_tm = StoreGate(K, "p_tm", 2, "scalar")
    K.r_ptf.rel = [K.phase_gate(), K.phase_gate()]
    ar_row = K.row("arrow", 16)
    d_ar = K.dma("scalar", ar_row[0:1, 0:4],
                 K.scr["cco_r2"][0:1, A2_AW:A2_AW + 4],
                 waits=[(K.cc, K.cc_ar2)])
    SC = 1.0 / (D * II)
    K.E("vector", lambda e: e.tensor_scalar(thr[0:1, 0:4], ar_row[0:1, 0:4],
                                            SC, None, ALU.mult), waits=[d_ar])
    K.E("vector", lambda e: e.tensor_scalar(thr[0:1, 4:8], thr[0:1, 0:4],
                                            EPS_TQ, ATANH_HALF, ALU.add,
                                            ALU.mult), waits=[K.ve.dep])
    K.E("vector", lambda e: e.tensor_scalar(thr[0:1, 8:12], thr[0:1, 4:8],
                                            -1.0, None, ALU.mult),
        waits=[K.ve.dep])
    thr_tp = K.row("thr_tp", 16)
    K.E("vector", lambda e: e.tensor_scalar(thr_tp[0:1, 4:8], thr[0:1, 0:4],
                                            1.0, None, ALU.add),
        waits=[K.ve.dep])
    K.E("vector", lambda e: e.tensor_scalar(thr_tp[0:1, 8:12], thr[0:1, 0:4],
                                            -1.0, 1.0, ALU.mult, ALU.add),
        waits=[K.ve.dep])
    K.E("vector", lambda e: e.reciprocal(thr_tp[0:1, 8:12], thr_tp[0:1, 8:12]),
        waits=[K.ve.dep])
    K.E("vector", lambda e: e.tensor_mul(thr_tp[0:1, 4:8], thr_tp[0:1, 4:8],
                                         thr_tp[0:1, 8:12]), waits=[K.ve.dep])
    K.E("scalar", lambda e: e.activation(thr_tp[0:1, 0:4], thr_tp[0:1, 4:8],
                                         AF.Ln), waits=[K.ve.dep])
    K.E("vector", lambda e: e.tensor_scalar(thr_tp[0:1, 0:4], thr_tp[0:1, 0:4],
                                            0.25, None, ALU.mult),
        waits=[K.ac.dep])
    K.thr_tp = thr_tp
    K.dep_thr_tp = K.ve.dep
    dep_thr = K.dma("scalar", K.scr["thr_d"][0:1, 0:12], thr[0:1, 0:12],
                    waits=[K.ve.dep])
    tcol_sem = K.sem("p_tcol")

    def ternarize(src, cci, wj, nrow, ncol):
        K.dma("gpsimd", tcol[1][:, :],
              K.scr["thr_d"][0:1, 4 + wj:5 + wj].broadcast_to([128, 1]),
              waits=[dep_thr, (K.ac, K.ac.n)], sem_override=tcol_sem)
        dep_tc = K.dma("gpsimd", tcol[0][:, :],
                       K.scr["thr_d"][0:1, 8 + wj:9 + wj].broadcast_to([128, 1]),
                       waits=[dep_thr], sem_override=tcol_sem)
        for r0 in range(0, nrow, 128):
            for c0 in range(0, ncol, 2048):
                sl, t_in, dep_in = K.r_ptf.fill(src[r0:r0 + 128, c0:c0 + 2048])
                K.E("scalar", lambda e, sl=sl, t=t_in:
                    e.activation(tm[sl][:, :], t[:, :], AF.Sign,
                                 bias=tcol[0][:, :]),
                    waits=[dep_in, dep_tc] + ternarize.rel[sl])
                K.E("scalar", lambda e, sl=sl, t=t_in:
                    e.activation(tsg[sl][:, :], t[:, :], AF.Sign,
                                 bias=tcol[1][:, :]), waits=[K.ac.dep])
                K.r_ptf.release(sl, K.ac.dep)
                K.E("vector", lambda e, sl=sl:
                    e.tensor_add(tm[sl][:, :], tm[sl][:, :], tsg[sl][:, :]),
                    waits=[K.ac.dep])
                ternarize.rel[sl] = [g_tm.store(
                    sl, cci[r0:r0 + 128, c0:c0 + 2048], tm[sl][:, :],
                    waits=[K.ve.dep])]
    ternarize.rel = [[], []]
    ternarize(K.inp["w1t_s"], K.scr["cci_m1"], 0, 256, II)
    ternarize(K.inp["w2t_s"], K.scr["cci_m2"], 1, 256, II)
    ternarize(K.inp["w3t_s"], K.scr["cci_m3"], 2, 1024, D)
    dep_m = [(sm, sm.n) for sm in g_tm.sems]
    K.cc_m1 = ag(K, "cci_m1", "cco_m1", dep_m)
    K.cc_m2 = ag(K, "cci_m2", "cco_m2", dep_m)
    K.cc_m3 = ag(K, "cci_m3", "cco_m3", dep_m)



class PairRing:

    def __init__(self, K, name, shape, offs, nslots, qeng, region="W"):
        self.K = K
        self.q = qeng
        alloc = {"A": K.vA, "B": K.vB, "W": K.vW}[region]
        self.hi = [alloc(shape, F32R, offs + i * 2 * _tb(shape), f"{name}_h{i}")
                   for i in range(nslots)]
        self.lo = [alloc(shape, F32R, offs + (i * 2 + 1) * _tb(shape), f"{name}_l{i}")
                   for i in range(nslots)]
        self.sems = [K.sem(f"pr_{name}{i}") for i in range(nslots)]
        self.rel = [[] for _ in range(nslots)]
        self.i = 0

    def fill(self, src_hi, src_lo, extra_waits=(), dslice=None):
        sl = self.i % len(self.hi)
        self.i += 1
        sem = self.sems[sl]
        w = list(self.rel[sl]) + list(extra_waits)

        def dst(t):
            a = t.ap().bitcast(F32)
            return a[:, :] if dslice is None else dslice(a)
        self.K.E(self.q, lambda e, d=dst(self.hi[sl]), s=src_hi:
                 e.dma_start(out=d, in_=s), waits=w, inc=(sem, 16))
        self.K.E(self.q, lambda e, d=dst(self.lo[sl]), s=src_lo:
                 e.dma_start(out=d, in_=s), waits=[], inc=(sem, 16))
        return sl, (sem, sem.n)

    def release(self, sl, *deps):
        self.rel[sl] = list(deps)


def _tb(shape):
    b = int(np.prod(shape[1:])) * 4
    return (b + 31) & ~31


def ph_l0attn_part1(K):
    nm_row = K.row("negmu", TOK)
    ri_row = K.row("r_a", TOK)
    rt_row = K.row("r_b", TOK)
    csv_row = K.row("csv", 512)
    xs_h = [K.vA((128, 512), F32R, kc * 2048, f"l0xh{kc}") for kc in range(NKC)]
    xs_l = [K.vA((128, 512), F32R, 32768 + kc * 2048, f"l0xl{kc}")
            for kc in range(NKC)]
    x2 = [K.vB((128, 512), F32, 0, "l0x2a"), K.vB((128, 512), F32, 2048, "l0x2b")]
    t1 = K.vB((128, 512), F32, 4096, "l0t1")
    qh_st = K.vB((128, 512), F32R, 6144, "l0qh")
    ql_st = K.vB((128, 512), F32R, 8192, "l0ql")
    bv_bc = K.vB((128, 512), F32, 10240, "l0bvbc")
    ri_col = [K.vB((128, 1), F32, 12288 + 32 * i, f"l0ric{i}") for i in range(4)]
    kf_st = K.vB((128, 512), F32, 12544, "l0kf")
    wring = PairRing(K, "l0w", (128, 128), 0, 4, "sync", region="W")
    wvring = PairRing(K, "l0wv", (128, 512), 4096, 2, "sync", region="W")
    g_act = StoreGate(K, "l0act", 4, "gpsimd")
    rowst = StoreGate(K, "l0row", 1, "scalar")
    csq_ring = [K.row("csst0", 128), K.row("csst1", 128)]
    cs_sems = [K.sem("l0cs0"), K.sem("l0cs1")]
    cs_rel = [[], []]

    dep_ldx = K.dep_x
    dep_bvbc = K.dma("gpsimd", bv_bc[:, :],
                     K.inp["bvr"][0:1, :].broadcast_to([128, 512]),
                     sem_override=K.sem("l0bvbc"))
    dep_csv = K.dma("scalar", csv_row[0:1, :],
                    K.scr["cco_r1"][0:1, A1_CSV:A1_CSV + 512],
                    waits=[(K.cc, K.cc_ar1)], sem_override=K.sem("l0csv"))

    evac_rel = []
    kf_rel = []

    for hf in range(2):
        hsl = slice(hf * 512, (hf + 1) * 512)
        for kc in range(NKC):
            xp = x2[kc % 2]
            K.E("scalar", lambda e, xp=xp, kc=kc, hsl=hsl:
                e.activation(xp[:, :], K.res[kc][:, hsl], AF.Square),
                waits=[dep_ldx, K.dep_zero] + ([K.mm.dep] if kc >= 2 else []))
            K.E("tensor", lambda e, kc=kc, hsl=hsl:
                e.matmul(K.ps[6][0:1, 0:512], K.ones_f[:, :], K.res[kc][:, hsl],
                         start=(kc == 0), stop=(kc == NKC - 1)),
                waits=[dep_ldx] + (K.psw(6) if kc == 0 else []))
            K.E("tensor", lambda e, xp=xp, kc=kc:
                e.matmul(K.ps[7][0:1, 0:512], K.ones_f[:, :], xp[:, :],
                         start=(kc == 0), stop=(kc == NKC - 1)),
                waits=[K.ac.dep] + (K.psw(7) if kc == 0 else []))
        MINV = -1.0 / D
        K.E("vector", lambda e, hsl=hsl: e.tensor_scalar(
            nm_row[0:1, hsl], K.ps[6][0:1, 0:512], MINV, None, ALU.mult),
            waits=[K.mm.dep, K.ve.dep])
        K.ps_done(6, K.ve.dep)
        K.E("vector", lambda e, hsl=hsl: e.tensor_scalar(
            ri_row[0:1, hsl], K.ps[7][0:1, 0:512], 1.0 / D, None, ALU.mult),
            waits=[K.ve.dep])
        K.ps_done(7, K.ve.dep)
        K.E("vector", lambda e, hsl=hsl: e.tensor_mul(
            rt_row[0:1, hsl], nm_row[0:1, hsl], nm_row[0:1, hsl]),
            waits=[K.ve.dep])
        K.E("vector", lambda e, hsl=hsl: e.tensor_sub(
            ri_row[0:1, hsl], ri_row[0:1, hsl], rt_row[0:1, hsl]),
            waits=[K.ve.dep])
        K.E("vector", lambda e, hsl=hsl: e.tensor_scalar(
            ri_row[0:1, hsl], ri_row[0:1, hsl], EPS_LN, None, ALU.add),
            waits=[K.ve.dep])
        K.E("scalar", lambda e, hsl=hsl: e.activation(
            ri_row[0:1, hsl], ri_row[0:1, hsl], AF.Sqrt), waits=[K.ve.dep])
        K.E("vector", lambda e, hsl=hsl: e.reciprocal(
            ri_row[0:1, hsl], ri_row[0:1, hsl]), waits=[K.ac.dep, K.ve.dep])
        dep_ri = K.ve.dep
        d_row = rowst.store(0, K.scr["row_d"][0:1, hf * 512:(hf + 1) * 512],
                            ri_row[0:1, hsl], waits=[dep_ri])
        d_rt = K.dma("gpsimd", K.bc[0][:, hsl],
                     K.scr["row_d"][0:1, hf * 512:(hf + 1) * 512]
                     .broadcast_to([128, 512]),
                     waits=[d_row], sem_override=K.sem("l0rt"))
        d_ric = []
        for sw in range(4):
            d_ric.append(K.dma(
                "gpsimd", ri_col[sw][:, :],
                K.scr["row_d"][0:1, hf * 512 + sw * 128:
                               hf * 512 + (sw + 1) * 128].rearrange(
                                   "a p -> p a"),
                waits=[d_row], sem_override=K.sem("l0ric")))
        for kc in range(NKC):
            K.E("vector", lambda e, kc=kc, hsl=hsl: e.tensor_copy(
                xs_h[kc][:, :], K.res[kc][:, hsl]),
                waits=([dep_ldx] + K.phase_gate()) if kc == 0 else [K.ve.dep])
            K.E("vector", lambda e, kc=kc, hsl=hsl: e.tensor_sub(
                xs_l[kc][:, :], K.res[kc][:, hsl],
                xs_h[kc].ap().bitcast(F32)[:, :]), waits=[K.ve.dep])
        dep_xs = K.ve.dep

        def proj(nmat, cco, nmc, aroff, bias_scl, out_hi, out_lo, out_f32,
                 cc_dep):
            for mc in range(nmc):
                bank = mc % 2
                csl = mc % 2
                d_cs = K.dma("scalar", csq_ring[csl][0:1, :],
                             K.scr["cco_r1"][0:1, aroff + mc * 128:
                                             aroff + (mc + 1) * 128],
                             waits=[(K.cc, K.cc_ar1)] + cs_rel[csl],
                             sem_override=cs_sems[csl])
                for kc in range(NKC):
                    sl, dep_w = wring.fill(
                        cco[kc // 2, 0, (kc % 2) * 128:(kc % 2 + 1) * 128,
                            mc * 128:(mc + 1) * 128],
                        cco[kc // 2, 1, (kc % 2) * 128:(kc % 2 + 1) * 128,
                            mc * 128:(mc + 1) * 128],
                        extra_waits=[(K.cc, cc_dep)])
                    first = (kc == 0)
                    K.E("tensor", lambda e, sl=sl, kc=kc, bank=bank:
                        e.matmul(K.ps[bank][:, :], wring.hi[sl][:, :],
                                 xs_h[kc][:, :], start=(kc == 0), stop=False),
                        waits=[dep_w, dep_xs] +
                              (K.psw(bank) if first else []))
                    K.E("tensor", lambda e, sl=sl, kc=kc, bank=bank:
                        e.matmul(K.ps[bank][:, :], wring.hi[sl][:, :],
                                 xs_l[kc][:, :], start=False, stop=False))
                    K.E("tensor", lambda e, sl=sl, kc=kc, bank=bank:
                        e.matmul(K.ps[bank][:, :], wring.lo[sl][:, :],
                                 xs_h[kc][:, :], start=False, stop=False))
                    wring.release(sl, K.mm.dep)
                K.E("tensor", lambda e, bank=bank, csl=csl, hsl=hsl:
                    e.matmul(K.ps[bank][:, :], csq_ring[csl][0:1, :],
                             nm_row[0:1, hsl], start=False, stop=True),
                    waits=[d_cs])
                cs_rel[csl] = [K.mm.dep]
                K.E("vector", lambda e, bank=bank, hsl=hsl:
                    e.tensor_mul(t1[:, :], K.ps[bank][:, :], K.bc[0][:, hsl]),
                    waits=[K.mm.dep, d_rt, K.ve.dep] + evac_rel)
                if out_f32 is None:
                    K.E("vector", lambda e, mc=mc:
                        e.tensor_scalar(qh_st[:, :], t1[:, :],
                                        bias_scl[:, mc:mc + 1], None, ALU.add),
                        waits=[K.ve.dep, K.dep_scl])
                    K.E("vector", lambda e, mc=mc:
                        e.scalar_tensor_tensor(
                            ql_st[:, :], t1[:, :], bias_scl[:, mc:mc + 1],
                            qh_st.ap().bitcast(F32)[:, :],
                            ALU.add, ALU.subtract), waits=[K.ve.dep])
                    K.ps_done(bank, K.ve.dep)
                    d1 = g_act.store(0, out_hi(mc, hf),
                                     qh_st.ap().bitcast(F32)[:, :],
                                     waits=[K.ve.dep])
                    d2 = g_act.store(1, out_lo(mc, hf),
                                     ql_st.ap().bitcast(F32)[:, :],
                                     waits=[K.ve.dep])
                    evac_rel[:] = [d1, d2]
                else:
                    K.E("vector", lambda e, mc=mc:
                        e.tensor_scalar(kf_st[:, :], t1[:, :],
                                        bias_scl[:, mc:mc + 1], None, ALU.add),
                        waits=[K.ve.dep, K.dep_scl] + kf_rel)
                    K.ps_done(bank, K.ve.dep)
                    kf_rel[:] = [g_act.store(2, out_f32(mc, hf), kf_st[:, :],
                                             waits=[K.ve.dep, K.dep_zero])]

        proj("q", K.scr["cco_pq"], 16, A1_CSQ, K.scl["bq"],
             lambda mc, hf: K.scr["q0h"][mc * 128:(mc + 1) * 128, hf * 512:(hf + 1) * 512],
             lambda mc, hf: K.scr["q0l"][mc * 128:(mc + 1) * 128, hf * 512:(hf + 1) * 512],
             None, K.cc_pq[1])
        proj("k", K.scr["cco_pk"], 4, A1_CSK, K.scl["bk"], None, None,
             lambda mc, hf: K.scr["cci_kv0"][mc * 128:(mc + 1) * 128,
                                             hf * 512:(hf + 1) * 512],
             K.cc_pk[1])

        for kc in range(NKC):
            sl, dep_w = wvring.fill(
                K.scr["cco_pv"][kc // 2, 0, (kc % 2) * 128:(kc % 2 + 1) * 128, :],
                K.scr["cco_pv"][kc // 2, 1, (kc % 2) * 128:(kc % 2 + 1) * 128, :],
                extra_waits=[(K.cc, K.cc_pv[1])])
            for sw in range(4):
                bank = 2 + sw
                first = (kc == 0)
                s0 = sw * 128
                K.E("tensor", lambda e, sl=sl, kc=kc, bank=bank, s0=s0:
                    e.matmul(K.ps[bank][:, :], xs_h[kc][:, s0:s0 + 128],
                             wvring.hi[sl][:, :], start=(kc == 0), stop=False),
                    waits=[dep_w, dep_xs] +
                          (K.psw(bank) if first else []))
                K.E("tensor", lambda e, sl=sl, kc=kc, bank=bank, s0=s0:
                    e.matmul(K.ps[bank][:, :], xs_h[kc][:, s0:s0 + 128],
                             wvring.lo[sl][:, :], start=False, stop=False))
                K.E("tensor", lambda e, sl=sl, kc=kc, bank=bank, s0=s0:
                    e.matmul(K.ps[bank][:, :], xs_l[kc][:, s0:s0 + 128],
                             wvring.hi[sl][:, :], start=False, stop=False))
            wvring.release(sl, K.mm.dep)
        for sw in range(4):
            bank = 2 + sw
            K.E("tensor", lambda e, bank=bank, sw=sw, hf=hf:
                e.matmul(K.ps[bank][:, :],
                         nm_row[0:1, hf * 512 + sw * 128:hf * 512 + (sw + 1) * 128],
                         csv_row[0:1, :], start=False, stop=True),
                waits=[dep_csv])
            K.E("vector", lambda e, bank=bank, sw=sw:
                e.tensor_scalar(t1[:, :], K.ps[bank][:, :],
                                ri_col[sw][:, 0:1], None, ALU.mult),
                waits=[K.mm.dep, d_ric[-1], K.ve.dep] + evac_rel)
            K.E("vector", lambda e:
                e.tensor_add(kf_st[:, :], t1[:, :], bv_bc[:, :]),
                waits=[K.ve.dep, dep_bvbc] + kf_rel)
            K.ps_done(bank, K.ve.dep)
            kf_rel[:] = [g_act.store(2, K.scr["cci_kv0"][
                512 + hf * 512 + sw * 128:512 + hf * 512 + (sw + 1) * 128,
                0:512], kf_st[:, :], waits=[K.ve.dep, K.dep_zero])]

    dep_kv = [(sm, sm.n) for sm in g_act.sems]
    K.cc_kv0 = ag(K, "cci_kv0", "cco_kv0", dep_kv, pairs=True)
    K.l0_g_act = g_act

def ph_l0attn_part2(K):
    kf = K.vB((128, 2048), F32, 0, "l0kf2")
    vtf = K.vB((128, 2048), F32, 8192, "l0vtf")
    dden = K.vB((128, 512), F32, 16384, "l0dden")
    t2 = K.vB((128, 512), F32, 18432, "l0t2")
    ah = K.vB((128, 512), F32R, 20480, "l0ah")
    al = K.vB((128, 512), F32R, 22528, "l0al")
    kh = K.vA((128, 2048), F32R, 0, "l0kh")
    kl = K.vA((128, 2048), F32R, 8192, "l0kl")
    vth = K.vA((128, 2048), F32R, 16384, "l0vth")
    vtl = K.vA((128, 2048), F32R, 24576, "l0vtl")
    qring = PairRing(K, "l0q", (128, 1024), 32768, 2, "sync", region="A")
    p32 = [K.vA((128, 512), F32, 49152, "l0p32a"),
           K.vA((128, 512), F32, 51200, "l0p32b")]
    phh = [K.vA((128, 512), F32R, 53248, "l0pha"),
           K.vA((128, 512), F32R, 55296, "l0phb")]
    pll = [K.vA((128, 512), F32R, 57344, "l0pla"),
           K.vA((128, 512), F32R, 59392, "l0plb")]
    g_act = K.l0_g_act
    rowst = StoreGate(K, "l0row2", 1, "scalar")
    den_row = K.row("r_b", TOK)
    kv_sem = K.sem("l0kvld")
    p_rel = [[], []]
    kv_rel = K.phase_gate()
    qring.rel = [K.phase_gate(), K.phase_gate()]
    smsh = K.vB((128, 1), F32, 24576, "l0smsh")
    K.E("vector", lambda e: e.memset(smsh[:, :], -SM_SHIFT), waits=K.phase_gate())
    K.smsh = smsh
    dden_sem = K.sem("l0dden")
    dden_rel = []

    for kv in range(4):
        d_k = []
        for p in range(2):
            d_k.append(K.dma("sync", kf[:, p * 1024:(p + 1) * 1024],
                             K.scr["cco_kv0"][p, kv * 128:(kv + 1) * 128, :],
                             waits=[(K.cc, K.cc_kv0[1])] + kv_rel,
                             sem_override=kv_sem))
            for i in range(8):
                d_k.append(K.dma(
                    "sync", vtf[:, (p * 8 + i) * 128:(p * 8 + i + 1) * 128],
                    K.scr["cco_kv0"][p, 512 + i * 128:512 + (i + 1) * 128,
                                     kv * 128:(kv + 1) * 128],
                    waits=[], sem_override=kv_sem))
        dep_kld = d_k[-1]
        K.E("vector", lambda e: e.tensor_copy(kh[:, :], kf[:, :]),
            waits=[dep_kld, K.ve.dep] + kv_rel)
        K.E("vector", lambda e: e.tensor_sub(
            kl[:, :], kf[:, :], kh.ap().bitcast(F32)[:, :]), waits=[K.ve.dep])
        K.E("vector", lambda e: e.tensor_copy(vth[:, :], vtf[:, :]),
            waits=[K.ve.dep])
        K.E("vector", lambda e: e.tensor_sub(
            vtl[:, :], vtf[:, :], vth.ap().bitcast(F32)[:, :]),
            waits=[K.ve.dep])
        dep_ksp = K.ve.dep
        kv_rel = [dep_ksp]
        for g in range(4):
            h = kv * 4 + g
            sl_q, dep_q = qring.fill(
                K.scr["q0h"][h * 128:(h + 1) * 128, :],
                K.scr["q0l"][h * 128:(h + 1) * 128, :],
                extra_waits=[(g_act.sems[0], g_act.sems[0].n),
                             (g_act.sems[1], g_act.sems[1].n)])
            for qt in range(2):
                qsl = slice(qt * 512, (qt + 1) * 512)
                for sc in range(16):
                    sb = sc % 2
                    pp = sc % 2
                    s0 = sc * 128
                    K.E("tensor", lambda e, sb=sb, s0=s0, sl_q=sl_q, qsl=qsl:
                        e.matmul(K.ps[sb][:, :], kh[:, s0:s0 + 128],
                                 qring.hi[sl_q][:, qsl], start=True, stop=False),
                        waits=[dep_ksp, dep_q] + K.psw(sb))
                    K.E("tensor", lambda e, sb=sb, s0=s0, sl_q=sl_q, qsl=qsl:
                        e.matmul(K.ps[sb][:, :], kh[:, s0:s0 + 128],
                                 qring.lo[sl_q][:, qsl], start=False, stop=False))
                    K.E("tensor", lambda e, sb=sb, s0=s0, sl_q=sl_q, qsl=qsl:
                        e.matmul(K.ps[sb][:, :], kl[:, s0:s0 + 128],
                                 qring.hi[sl_q][:, qsl], start=False, stop=True))
                    dep_sc = K.mm.dep
                    K.E("scalar", lambda e, sb=sb, pp=pp:
                        e.activation(p32[pp][:, :], K.ps[sb][:, :], AF.Exp,
                                     bias=smsh[:, 0:1], scale=ISQ_DH),
                        waits=[dep_sc] + p_rel[pp])
                    K.ps_done(sb, K.ac.dep)
                    K.E("vector", lambda e, pp=pp: e.tensor_copy(
                        phh[pp][:, :], p32[pp][:, :]), waits=[K.ac.dep])
                    K.E("vector", lambda e, pp=pp: e.tensor_sub(
                        pll[pp][:, :], p32[pp][:, :],
                        phh[pp].ap().bitcast(F32)[:, :]), waits=[K.ve.dep])
                    dep_psp = K.ve.dep
                    K.E("tensor", lambda e, pp=pp, sc=sc:
                        e.matmul(K.ps[6][0:1, 0:512], K.ones_r[:, :],
                                 phh[pp][:, :], start=(sc == 0), stop=False),
                        waits=[dep_psp] + (K.psw(6) if sc == 0 else []))
                    K.E("tensor", lambda e, pp=pp, sc=sc:
                        e.matmul(K.ps[6][0:1, 0:512], K.ones_r[:, :],
                                 pll[pp][:, :], start=False,
                                 stop=(sc == 15)))
                    K.E("tensor", lambda e, pp=pp, sc=sc, s0=s0:
                        e.matmul(K.ps[2][:, :], vth[:, s0:s0 + 128],
                                 phh[pp][:, :], start=(sc == 0), stop=False),
                        waits=(K.psw(2) if sc == 0 else []))
                    K.E("tensor", lambda e, pp=pp, sc=sc, s0=s0:
                        e.matmul(K.ps[2][:, :], vth[:, s0:s0 + 128],
                                 pll[pp][:, :], start=False, stop=False))
                    K.E("tensor", lambda e, pp=pp, sc=sc, s0=s0:
                        e.matmul(K.ps[2][:, :], vtl[:, s0:s0 + 128],
                                 phh[pp][:, :], start=False, stop=(sc == 15)))
                    p_rel[pp] = [K.mm.dep]
                K.E("vector", lambda e, qsl=qsl: e.tensor_copy(
                    den_row[0:1, 0:512], K.ps[6][0:1, 0:512]),
                    waits=[K.mm.dep, K.ve.dep])
                K.ps_done(6, K.ve.dep)
                K.E("vector", lambda e: e.reciprocal(
                    den_row[0:1, 0:512], den_row[0:1, 0:512]),
                    waits=[K.ve.dep])
                d_dr = rowst.store(0, K.scr["row_d"][0:1, 0:512],
                                   den_row[0:1, 0:512], waits=[K.ve.dep])
                d_db = K.dma("gpsimd", dden[:, :],
                             K.scr["row_d"][0:1, 0:512].broadcast_to([128, 512]),
                             waits=[d_dr] + dden_rel, sem_override=dden_sem)
                K.E("vector", lambda e: e.tensor_mul(
                    t2[:, :], K.ps[2][:, :], dden[:, :]),
                    waits=[K.mm.dep, d_db, K.ve.dep])
                K.ps_done(2, K.ve.dep)
                dden_rel = [K.ve.dep]
                K.E("vector", lambda e: e.tensor_copy(ah[:, :], t2[:, :]),
                    waits=[K.ve.dep] + [(g_act.sems[0], g_act.sems[0].n)])
                K.E("vector", lambda e: e.tensor_sub(
                    al[:, :], t2[:, :], ah.ap().bitcast(F32)[:, :]),
                    waits=[K.ve.dep, (g_act.sems[1], g_act.sems[1].n)])
                g_act.store(0, K.scr["a0h"][h * 128:(h + 1) * 128, qsl],
                            ah.ap().bitcast(F32)[:, :], waits=[K.ve.dep])
                g_act.store(1, K.scr["a0l"][h * 128:(h + 1) * 128, qsl],
                            al.ap().bitcast(F32)[:, :], waits=[K.ve.dep])
            qring.release(sl_q, K.mm.dep)


def ph_l0attn_part3(K):
    a_h = [K.vA((128, 512), F32R, kc * 2048, f"l0ah{kc}") for kc in range(NKC)]
    a_l = [K.vA((128, 512), F32R, 32768 + kc * 2048, f"l0al{kc}")
           for kc in range(NKC)]
    woring = PairRing(K, "l0wo", (128, 128), 0, 4, "sync", region="W")
    woring.rel = [K.phase_gate() for _ in range(4)]
    ald_sem = K.sem("l0ald")
    g_act = K.l0_g_act
    a_gate = [(g_act.sems[0], g_act.sems[0].n), (g_act.sems[1], g_act.sems[1].n)]
    a_rel = K.phase_gate()
    for qt in range(2):
        qsl = slice(qt * 512, (qt + 1) * 512)
        d_a = None
        for kc in range(NKC):
            K.dma("sync", a_h[kc].ap().bitcast(F32)[:, :],
                  K.scr["a0h"][kc * 128:(kc + 1) * 128, qsl],
                  waits=a_gate + a_rel, sem_override=ald_sem)
            d_a = K.dma("sync", a_l[kc].ap().bitcast(F32)[:, :],
                        K.scr["a0l"][kc * 128:(kc + 1) * 128, qsl],
                        waits=[], sem_override=ald_sem)
        for dc in range(NKC):
            bank = dc % 2
            for kc in range(NKC):
                sl, dep_w = woring.fill(
                    K.scr["cco_po"][kc // 2, 0, (kc % 2) * 128:(kc % 2 + 1) * 128,
                                    dc * 128:(dc + 1) * 128],
                    K.scr["cco_po"][kc // 2, 1, (kc % 2) * 128:(kc % 2 + 1) * 128,
                                    dc * 128:(dc + 1) * 128],
                    extra_waits=[(K.cc, K.cc_po[1])])
                first = (kc == 0)
                K.E("tensor", lambda e, sl=sl, kc=kc, bank=bank:
                    e.matmul(K.ps[bank][:, :], woring.hi[sl][:, :],
                             a_h[kc][:, :], start=(kc == 0), stop=False),
                    waits=[dep_w, d_a] + (K.psw(bank) if first else []))
                K.E("tensor", lambda e, sl=sl, kc=kc, bank=bank:
                    e.matmul(K.ps[bank][:, :], woring.hi[sl][:, :],
                             a_l[kc][:, :], start=False, stop=False))
                K.E("tensor", lambda e, sl=sl, kc=kc, bank=bank:
                    e.matmul(K.ps[bank][:, :], woring.lo[sl][:, :],
                             a_h[kc][:, :], start=False, stop=(kc == NKC - 1)))
                woring.release(sl, K.mm.dep)
            K.E("vector", lambda e, bank=bank, dc=dc, qsl=qsl:
                e.scalar_tensor_tensor(
                    K.res[dc][:, qsl], K.ps[bank][:, :], K.scl["bo"][:, dc:dc + 1],
                    K.res[dc][:, qsl], ALU.add, ALU.add),
                waits=[K.mm.dep, K.ve.dep, K.dep_scl])
            K.ps_done(bank, K.ve.dep)
        a_rel = [K.mm.dep]

def build(upto=99, debug=None, skip_l1p3=False, dbg_l1ffn=False):
    K = Kern(upto)
    K.skip_l1p3 = skip_l1p3
    K.dbg_l1ffn = dbg_l1ffn
    ph_init(K)
    if upto >= 1:
        ph_prep_l0attn_w(K)
    if upto >= 2:
        ph_l0attn_part1(K)
        ph_prep_wo(K)
        ph_prep_gud(K)
        ph_l0attn_part2(K)
        ph_l0attn_part3(K)
    if upto >= 3:
        ph_prep_l1attn_w(K)
        ph_l0ffn(K)
    if upto >= 4:
        ph_l1attn(K)
    if upto >= 5:
        ph_prep_tern(K)
        ph_l1ffn(K)
    if debug is not None:
        debug(K)
    dep_last = K.phase_gate()
    ost = K.sem("out_st")
    for k in range(NKC):
        K.dma("sync", K.out[k * 128:(k + 1) * 128, :], K.res[k][:, :],
              waits=dep_last + [K.dep_x], sem_override=ost)
    K.E("gpsimd", lambda e, h=ost.h, v=ost.n: e.wait_ge(h, v), inc=False)
    with K.nc.Block() as block:
        for e in ENGINES:
            qe = K.q[e]

            def runner(eng, qe=qe):
                for f in qe:
                    f(eng)
            getattr(block, e)(runner)
    return K


def kernel(**inputs):
    common, per_core = host_prepare(inputs)
    K = build()
    in_maps = [dict(common, **pc) for pc in per_core]
    res = run_bass_kernel_spmd(K.nc, in_maps, core_ids=list(range(8)))
    out = np.empty((B, S, D), dtype=np.float32)
    for c in range(8):
        b, half = c // 2, c % 2
        o = np.asarray(res.results[c]["out"])
        out[b, half * TOK:(half + 1) * TOK, :] = o.T
    return out


def ph_l0ffn(K):
    nc = K.nc
    at = nc.alloc_sbuf_tensor_at
    hhi = [at(f"f0hh{ic}", [128, 512], BF16, offset=K.OFF_A + ic * 1024)
           for ic in range(NIC)]
    xbh = [at(f"f0xh{kc}", [128, 512], BF16, offset=K.OFF_B + kc * 1024)
           for kc in range(NKC)]
    xbl = [at(f"f0xl{kc}", [128, 512], BF16, offset=K.OFF_B + 16384 + kc * 1024)
           for kc in range(NKC)]
    dacc = [at(f"f0da{dc}", [128, 512], F32, offset=K.OFF_B + dc * 2048)
            for dc in range(NKC)]
    gcol = [at(f"f0gc{i}", [128, 1024], BF16, offset=K.OFF_W + i * 2048)
            for i in range(2)]
    ucol = [at(f"f0uc{i}", [128, 1024], BF16, offset=K.OFF_W + 4096 + i * 2048)
            for i in range(2)]
    asil = at("f0asil", [128, 512], F32, offset=K.OFF_W + 8192)
    h32 = at("f0h32", [128, 512], F32, offset=K.OFF_W + 10240)
    hlst = at("f0hlst", [128, 512], BF16, offset=K.OFF_W + 12288)
    dcol = [at(f"f0dc{i}", [128, 4096], BF16, offset=K.OFF_W + i * 8192)
            for i in range(2)]
    x2t = at("f0x2", [128, 512], F32, offset=K.OFF_BC + 4096)
    nm_row = K.row("negmu", TOK)
    ri_row = K.row("r_a", TOK)
    rt_row = K.row("r_b", TOK)
    nmri = rt_row
    csst = [K.row("csst0", 128), K.row("csst1", 128)]
    csstu = [K.row("csstu0", 128), K.row("csstu1", 128)]
    cs_sems = [K.sem("f0cs0"), K.sem("f0cs1")]
    csu_sems = [K.sem("f0csu0"), K.sem("f0csu1")]
    cs_rel = [K.phase_gate(), K.phase_gate()]
    csu_rel = [K.phase_gate(), K.phase_gate()]
    rowst = StoreGate(K, "f0row", 1, "scalar")
    g_hl = StoreGate(K, "f0hl", 1, "gpsimd")
    r_g = Ring(K, "f0g", gcol, "sync")
    r_u = Ring(K, "f0u", ucol, "sync")
    r_d = Ring(K, "f0d", dcol, "sync")
    rt_sem = K.sem("f0rt")

    for til in range(2):
        tsl = slice(til * 512, (til + 1) * 512)
        gate0 = K.phase_gate()
        hl_ld_prev = gate0
        for kc in range(NKC):
            K.E("scalar", lambda e, kc=kc, tsl=tsl:
                e.activation(x2t[:, :], K.res[kc][:, tsl], AF.Square),
                waits=gate0 + [K.mm.dep])
            K.E("tensor", lambda e, kc=kc, tsl=tsl:
                e.matmul(K.ps[6][0:1, 0:512], K.ones_f[:, :], K.res[kc][:, tsl],
                         start=(kc == 0), stop=(kc == NKC - 1)),
                waits=gate0 + (K.psw(6) if kc == 0 else []))
            K.E("tensor", lambda e, kc=kc:
                e.matmul(K.ps[7][0:1, 0:512], K.ones_f[:, :], x2t[:, :],
                         start=(kc == 0), stop=(kc == NKC - 1)),
                waits=[K.ac.dep] + (K.psw(7) if kc == 0 else []))
        MINV = -1.0 / D
        K.E("vector", lambda e, tsl=tsl: e.tensor_scalar(
            nm_row[0:1, tsl], K.ps[6][0:1, 0:512], MINV, None, ALU.mult),
            waits=[K.mm.dep, K.ve.dep])
        K.ps_done(6, K.ve.dep)
        K.E("vector", lambda e, tsl=tsl: e.tensor_scalar(
            ri_row[0:1, tsl], K.ps[7][0:1, 0:512], 1.0 / D, None, ALU.mult),
            waits=[K.ve.dep])
        K.ps_done(7, K.ve.dep)
        K.E("vector", lambda e, tsl=tsl: e.tensor_mul(
            rt_row[0:1, tsl], nm_row[0:1, tsl], nm_row[0:1, tsl]),
            waits=[K.ve.dep])
        K.E("vector", lambda e, tsl=tsl: e.tensor_sub(
            ri_row[0:1, tsl], ri_row[0:1, tsl], rt_row[0:1, tsl]),
            waits=[K.ve.dep])
        K.E("vector", lambda e, tsl=tsl: e.tensor_scalar(
            ri_row[0:1, tsl], ri_row[0:1, tsl], EPS_LN, None, ALU.add),
            waits=[K.ve.dep])
        K.E("scalar", lambda e, tsl=tsl: e.activation(
            ri_row[0:1, tsl], ri_row[0:1, tsl], AF.Sqrt), waits=[K.ve.dep])
        K.E("vector", lambda e, tsl=tsl: e.reciprocal(
            ri_row[0:1, tsl], ri_row[0:1, tsl]), waits=[K.ac.dep, K.ve.dep])
        K.E("vector", lambda e, tsl=tsl: e.tensor_mul(
            nmri[0:1, tsl], nm_row[0:1, tsl], ri_row[0:1, tsl]),
            waits=[K.ve.dep])
        dep_ri = K.ve.dep
        d_row = rowst.store(0, K.scr["row_d"][0:1, tsl], ri_row[0:1, tsl],
                            waits=[dep_ri] + (gate0 if til == 0 else []))
        d_rt = K.dma("gpsimd", K.bc[0][:, tsl],
                     K.scr["row_d"][0:1, tsl].broadcast_to([128, 512]),
                     waits=[d_row], sem_override=rt_sem)
        for kc in range(NKC):
            K.E("vector", lambda e, kc=kc, tsl=tsl: e.tensor_mul(
                x2t[:, :], K.res[kc][:, tsl], K.bc[0][:, tsl]),
                waits=[d_rt, K.ve.dep, K.ac.dep])
            K.E("vector", lambda e, kc=kc: e.tensor_copy(
                xbh[kc][:, :], x2t[:, :]), waits=[K.ve.dep])
            K.E("vector", lambda e, kc=kc: e.tensor_sub(
                xbl[kc][:, :], x2t[:, :], xbh[kc][:, :]), waits=[K.ve.dep])
        dep_xs = K.ve.dep
        r_g.rel = [K.phase_gate(), K.phase_gate()]
        r_u.rel = [K.phase_gate(), K.phase_gate()]
        hl_ld_gate = (K.sem("f0hlin"), K.sem("f0hlin").n)
        hl_rel = []
        for ic in range(NIC):
            def colsrc(cco, r0, ic):
                return cco[r0:r0 + 1024, ic * 128:(ic + 1) * 128].rearrange(
                    "(c p) i -> p c i", p=128)
            cslice = lambda t: t.ap().rearrange("p (c i) -> p c i", i=128)
            sg, tg, dep_g0 = r_g.fill(colsrc(K.scr["cco_g"], 0, ic),
                                      tile_slice=cslice,
                                      extra_waits=[(K.cc, K.cc_gate[1])])
            sg2, tg2, dep_g1 = r_g.fill(colsrc(K.scr["cco_g"], 1024, ic),
                                        tile_slice=cslice)
            su, tu, dep_u0 = r_u.fill(colsrc(K.scr["cco_u"], 0, ic),
                                      tile_slice=cslice,
                                      extra_waits=[(K.cc, K.cc_up[1])])
            su2, tu2, dep_u1 = r_u.fill(colsrc(K.scr["cco_u"], 1024, ic),
                                        tile_slice=cslice)
            d_cs = K.dma("scalar", csst[ic % 2][0:1, :],
                         K.scr["cco_r2"][0:1, A2_CSG + ic * 128:
                                         A2_CSG + (ic + 1) * 128],
                         waits=[(K.cc, K.cc_ar2)] + cs_rel[ic % 2],
                         sem_override=cs_sems[ic % 2])
            d_csu = K.dma("scalar", csstu[ic % 2][0:1, :],
                          K.scr["cco_r2"][0:1, A2_CSU + ic * 128:
                                          A2_CSU + (ic + 1) * 128],
                          waits=csu_rel[ic % 2], sem_override=csu_sems[ic % 2])
            for half, (tgx, tux, depg, depu) in enumerate(
                    ((tg, tu, dep_g0, dep_u0), (tg2, tu2, dep_g1, dep_u1))):
                for k8 in range(8):
                    kc = half * 8 + k8
                    first = (kc == 0)
                    K.E("tensor", lambda e, tgx=tgx, k8=k8, kc=kc:
                        e.matmul(K.ps[0][:, :], tgx[:, k8 * 128:(k8 + 1) * 128],
                                 xbh[kc][:, :], start=(kc == 0), stop=False),
                        waits=[depg, dep_xs] + (K.psw(0) if first else []))
                    K.E("tensor", lambda e, tgx=tgx, k8=k8, kc=kc:
                        e.matmul(K.ps[0][:, :], tgx[:, k8 * 128:(k8 + 1) * 128],
                                 xbl[kc][:, :], start=False, stop=False))
                    K.E("tensor", lambda e, tux=tux, k8=k8, kc=kc:
                        e.matmul(K.ps[1][:, :], tux[:, k8 * 128:(k8 + 1) * 128],
                                 xbh[kc][:, :], start=(kc == 0), stop=False),
                        waits=[depu] + (K.psw(1) if first else []))
                    K.E("tensor", lambda e, tux=tux, k8=k8, kc=kc:
                        e.matmul(K.ps[1][:, :], tux[:, k8 * 128:(k8 + 1) * 128],
                                 xbl[kc][:, :], start=False, stop=False))
            K.E("tensor", lambda e, ic=ic, tsl=tsl:
                e.matmul(K.ps[0][:, :], csst[ic % 2][0:1, :], nmri[0:1, tsl],
                         start=False, stop=True),
                waits=[d_cs])
            K.E("tensor", lambda e, ic=ic, tsl=tsl:
                e.matmul(K.ps[1][:, :], csstu[ic % 2][0:1, :], nmri[0:1, tsl],
                         start=False, stop=True),
                waits=[d_csu])
            cs_rel[ic % 2] = [K.mm.dep]
            csu_rel[ic % 2] = [K.mm.dep]
            r_g.release(sg, K.mm.dep)
            r_g.release(sg2, K.mm.dep)
            r_u.release(su, K.mm.dep)
            r_u.release(su2, K.mm.dep)
            K.E("scalar", lambda e, ic=ic:
                e.activation(asil[:, :], K.ps[0][:, :], AF.Silu,
                             scale=K.scl["gs"][:, ic:ic + 1]),
                waits=[K.mm.dep, K.ve.dep, K.dep_scl])
            K.ps_done(0, K.ac.dep)
            K.E("vector", lambda e, ic=ic:
                e.scalar_tensor_tensor(h32[:, :], K.ps[1][:, :],
                                       K.scl["us"][:, ic:ic + 1], asil[:, :],
                                       ALU.mult, ALU.mult),
                waits=[K.ac.dep, K.ve.dep])
            K.ps_done(1, K.ve.dep)
            K.E("vector", lambda e, ic=ic: e.tensor_copy(
                hhi[ic][:, :], h32[:, :]), waits=[K.ve.dep] + (gate0 if ic < 2 else []))
            K.E("vector", lambda e, ic=ic: e.tensor_sub(
                hlst[:, :], h32[:, :], hhi[ic][:, :]),
                waits=[K.ve.dep] + hl_rel)
            hl_rel = [g_hl.store(0, K.scr["hlo0"][ic * 128:(ic + 1) * 128, :],
                                 hlst[:, :], waits=[K.ve.dep, hl_ld_gate])]
        dep_hl_all = (g_hl.sems[0], g_hl.sems[0].n)
        for pss in range(2):
            if pss == 1:
                hl_sem = K.sem("f0hlin")
                gate_l = [K.mm.dep, dep_hl_all]
                for ic in range(NIC):
                    K.dma("sync", hhi[ic][:, :],
                          K.scr["hlo0"][ic * 128:(ic + 1) * 128, :],
                          waits=(gate_l if ic == 0 else []),
                          sem_override=hl_sem)
                dep_hlin = (hl_sem, hl_sem.n)
            for dc in range(NKC):
                bank = 4 + dc % 2
                dhalf = []
                for i2 in range(2):
                    sl_d, td, dep_d = r_d.fill(
                        K.scr["cco_d"][i2 * 4096:(i2 + 1) * 4096,
                                       dc * 128:(dc + 1) * 128].rearrange(
                            "(c p) i -> p c i", p=128),
                        tile_slice=lambda t: t.ap().rearrange(
                            "p (c i) -> p c i", i=128),
                        extra_waits=[(K.cc, K.cc_down[1])] +
                                    (K.phase_gate() if (pss == 0 and dc == 0)
                                     else []))
                    dhalf.append((sl_d, td, dep_d))
                if pss == 0:
                    for ic in range(NIC):
                        sl_d, td, dep_d = dhalf[ic // 32]
                        i32 = ic % 32
                        K.E("tensor", lambda e, td=td, i32=i32, ic=ic, bank=bank:
                            e.matmul(K.ps[bank][:, :],
                                     td[:, i32 * 128:(i32 + 1) * 128],
                                     hhi[ic][:, :], start=(ic == 0),
                                     stop=(ic == NIC - 1)),
                            waits=[dep_d] + (K.psw(bank) if ic == 0 else []))
                    K.E("vector", lambda e, dc=dc, bank=bank: e.tensor_copy(
                        dacc[dc][:, :], K.ps[bank][:, :]),
                        waits=[K.mm.dep, K.ve.dep] +
                              (K.phase_gate() if dc == 0 else []))
                    K.ps_done(bank, K.ve.dep)
                else:
                    for ic in range(NIC):
                        sl_d, td, dep_d = dhalf[ic // 32]
                        i32 = ic % 32
                        K.E("tensor", lambda e, td=td, i32=i32, ic=ic,
                            bank=bank:
                            e.matmul(K.ps[bank][:, :],
                                     td[:, i32 * 128:(i32 + 1) * 128],
                                     hhi[ic][:, :], start=(ic == 0),
                                     stop=(ic == NIC - 1)),
                            waits=[dep_d, dep_hlin] +
                                  (K.psw(bank) if ic == 0 else []))
                    K.E("vector", lambda e, dc=dc, bank=bank: e.tensor_add(
                        dacc[dc][:, :], dacc[dc][:, :], K.ps[bank][:, :]),
                        waits=[K.mm.dep, K.ve.dep])
                    K.ps_done(bank, K.ve.dep)
                    K.E("vector", lambda e, dc=dc, tsl=tsl:
                        e.scalar_tensor_tensor(
                            K.res[dc][:, tsl], dacc[dc][:, :],
                            K.scl["ds"][:, dc:dc + 1], K.res[dc][:, tsl],
                            ALU.mult, ALU.add),
                        waits=[K.ve.dep, K.dep_scl])
                for i2 in range(2):
                    r_d.release(dhalf[i2][0], K.mm.dep)


def ph_l1attn(K):
    nc = K.nc
    at = nc.alloc_sbuf_tensor_at
    nm_row = K.row("negmu", TOK)
    ri_row = K.row("r_a", TOK)
    rt_row = K.row("r_b", TOK)
    xs_h = [at(f"l1xh{kc}", [128, 512], F32R, offset=K.OFF_A + kc * 2048)
            for kc in range(NKC)]
    xs_l = [at(f"l1xl{kc}", [128, 512], F32R, offset=K.OFF_A + 32768 + kc * 2048)
            for kc in range(NKC)]
    x2 = [K.vB((128, 512), F32, 0, "l1x2a"), K.vB((128, 512), F32, 2048, "l1x2b")]
    t1 = K.vB((128, 512), F32, 4096, "l1t1")
    qh_st = K.vB((128, 512), F32R, 6144, "l1qh")
    ql_st = K.vB((128, 512), F32R, 8192, "l1ql")
    kf_st = K.vB((128, 512), F32, 10240, "l1kf")
    wring = PairRing(K, "l0w", (128, 128), 0, 4, "sync", region="W")
    wring.rel = [K.phase_gate() for _ in range(4)]
    wvring = PairRing(K, "l0wv", (128, 512), 4096, 2, "sync", region="W")
    wvring.rel = [K.phase_gate() for _ in range(2)]
    g_act = StoreGate(K, "l0act", 4, "gpsimd")
    rowst = StoreGate(K, "l0row", 1, "scalar")
    csq_ring = [K.row("csst0", 128), K.row("csst1", 128)]
    cs_sems = [K.sem("l0cs0"), K.sem("l0cs1")]
    cs_rel = [K.phase_gate(), K.phase_gate()]
    rt_sem = K.sem("l0rt")
    evac_rel = K.phase_gate()
    kf_rel = K.phase_gate()
    dep_ldx = K.dep_x
    ri_col = [K.vB((128, 1), F32, 12288 + 32 * i, f"l1ric{i}")
              for i in range(4)]

    for hf in range(2):
        hsl = slice(hf * 512, (hf + 1) * 512)
        gate0 = K.phase_gate()
        for kc in range(NKC):
            xp = x2[kc % 2]
            K.E("scalar", lambda e, xp=xp, kc=kc, hsl=hsl:
                e.activation(xp[:, :], K.res[kc][:, hsl], AF.Square),
                waits=gate0 + ([K.mm.dep] if kc >= 2 else []))
            K.E("tensor", lambda e, kc=kc, hsl=hsl:
                e.matmul(K.ps[6][0:1, 0:512], K.ones_f[:, :], K.res[kc][:, hsl],
                         start=(kc == 0), stop=(kc == NKC - 1)),
                waits=gate0 + (K.psw(6) if kc == 0 else []))
            K.E("tensor", lambda e, xp=xp, kc=kc:
                e.matmul(K.ps[7][0:1, 0:512], K.ones_f[:, :], xp[:, :],
                         start=(kc == 0), stop=(kc == NKC - 1)),
                waits=[K.ac.dep] + (K.psw(7) if kc == 0 else []))
        MINV = -1.0 / D
        K.E("vector", lambda e, hsl=hsl: e.tensor_scalar(
            nm_row[0:1, hsl], K.ps[6][0:1, 0:512], MINV, None, ALU.mult),
            waits=[K.mm.dep, K.ve.dep])
        K.ps_done(6, K.ve.dep)
        K.E("vector", lambda e, hsl=hsl: e.tensor_scalar(
            ri_row[0:1, hsl], K.ps[7][0:1, 0:512], 1.0 / D, None, ALU.mult),
            waits=[K.ve.dep])
        K.ps_done(7, K.ve.dep)
        K.E("vector", lambda e, hsl=hsl: e.tensor_mul(
            rt_row[0:1, hsl], nm_row[0:1, hsl], nm_row[0:1, hsl]),
            waits=[K.ve.dep])
        K.E("vector", lambda e, hsl=hsl: e.tensor_sub(
            ri_row[0:1, hsl], ri_row[0:1, hsl], rt_row[0:1, hsl]),
            waits=[K.ve.dep])
        K.E("vector", lambda e, hsl=hsl: e.tensor_scalar(
            ri_row[0:1, hsl], ri_row[0:1, hsl], EPS_LN, None, ALU.add),
            waits=[K.ve.dep])
        K.E("scalar", lambda e, hsl=hsl: e.activation(
            ri_row[0:1, hsl], ri_row[0:1, hsl], AF.Sqrt), waits=[K.ve.dep])
        K.E("vector", lambda e, hsl=hsl: e.reciprocal(
            ri_row[0:1, hsl], ri_row[0:1, hsl]), waits=[K.ac.dep, K.ve.dep])
        dep_ri = K.ve.dep
        d_row = rowst.store(0, K.scr["row_d"][0:1, hsl], ri_row[0:1, hsl],
                            waits=[dep_ri] + (gate0 if hf == 0 else []))
        d_rt = K.dma("gpsimd", K.bc[0][:, hsl],
                     K.scr["row_d"][0:1, hsl].broadcast_to([128, 512]),
                     waits=[d_row], sem_override=rt_sem)
        d_ric = []
        for sw in range(4):
            d_ric.append(K.dma(
                "gpsimd", ri_col[sw][:, :],
                K.scr["row_d"][0:1, hf * 512 + sw * 128:
                               hf * 512 + (sw + 1) * 128].rearrange("a p -> p a"),
                waits=[d_row], sem_override=K.sem("l0ric")))
        for kc in range(NKC):
            K.E("vector", lambda e, kc=kc, hsl=hsl: e.tensor_copy(
                xs_h[kc][:, :], K.res[kc][:, hsl]),
                waits=([dep_ldx] + K.phase_gate()) if kc == 0 else [K.ve.dep])
            K.E("vector", lambda e, kc=kc, hsl=hsl: e.tensor_sub(
                xs_l[kc][:, :], K.res[kc][:, hsl],
                xs_h[kc].ap().bitcast(F32)[:, :]), waits=[K.ve.dep])
        dep_xs = K.ve.dep

        for h in range(16):
            for which in ("q", "k"):
                cco = K.scr["cco_pq1" if which == "q" else "cco_pk1"]
                aroff = (A2_CSQ1 if which == "q" else A2_CSK1) + h * 128
                bank = h % 2
                csl = h % 2
                d_cs = K.dma("scalar", csq_ring[csl][0:1, :],
                             K.scr["cco_r2"][0:1, aroff:aroff + 128],
                             waits=[(K.cc, K.cc_ar2)] + cs_rel[csl],
                             sem_override=cs_sems[csl])
                for kc in range(NKC):
                    sl, dep_w = wring.fill(
                        cco[kc // 2, 0, h, (kc % 2) * 128:(kc % 2 + 1) * 128, :],
                        cco[kc // 2, 1, h, (kc % 2) * 128:(kc % 2 + 1) * 128, :],
                        extra_waits=[(K.cc, K.cc_pq1[1] if which == "q"
                                      else K.cc_pk1[1])])
                    first = (kc == 0)
                    K.E("tensor", lambda e, sl=sl, kc=kc, bank=bank:
                        e.matmul(K.ps[bank][:, :], wring.hi[sl][:, :],
                                 xs_h[kc][:, :], start=(kc == 0), stop=False),
                        waits=[dep_w, dep_xs] + (K.psw(bank) if first else []))
                    K.E("tensor", lambda e, sl=sl, kc=kc, bank=bank:
                        e.matmul(K.ps[bank][:, :], wring.hi[sl][:, :],
                                 xs_l[kc][:, :], start=False, stop=False))
                    K.E("tensor", lambda e, sl=sl, kc=kc, bank=bank:
                        e.matmul(K.ps[bank][:, :], wring.lo[sl][:, :],
                                 xs_h[kc][:, :], start=False, stop=False))
                    wring.release(sl, K.mm.dep)
                K.E("tensor", lambda e, bank=bank, csl=csl, hsl=hsl:
                    e.matmul(K.ps[bank][:, :], csq_ring[csl][0:1, :],
                             nm_row[0:1, hsl], start=False, stop=True),
                    waits=[d_cs])
                cs_rel[csl] = [K.mm.dep]
                K.E("vector", lambda e, bank=bank, hsl=hsl:
                    e.tensor_mul(t1[:, :], K.ps[bank][:, :], K.bc[0][:, hsl]),
                    waits=[K.mm.dep, d_rt, K.ve.dep] + evac_rel)
                if which == "q":
                    K.E("vector", lambda e: e.tensor_copy(qh_st[:, :], t1[:, :]),
                        waits=[K.ve.dep])
                    K.E("vector", lambda e: e.tensor_sub(
                        ql_st[:, :], t1[:, :],
                        qh_st.ap().bitcast(F32)[:, :]), waits=[K.ve.dep])
                    K.ps_done(bank, K.ve.dep)
                    d1 = g_act.store(0, K.scr["q1h"][h * 128:(h + 1) * 128, hsl],
                                     qh_st.ap().bitcast(F32)[:, :],
                                     waits=[K.ve.dep])
                    d2 = g_act.store(1, K.scr["q1l"][h * 128:(h + 1) * 128, hsl],
                                     ql_st.ap().bitcast(F32)[:, :],
                                     waits=[K.ve.dep])
                    evac_rel = [d1, d2]
                else:
                    K.E("vector", lambda e: e.tensor_copy(kf_st[:, :], t1[:, :]),
                        waits=[K.ve.dep] + kf_rel)
                    K.ps_done(bank, K.ve.dep)
                    kf_rel = [g_act.store(
                        2, K.scr["cci_kv1"][(h // 2) * 128:(h // 2 + 1) * 128,
                                            (h % 2) * 1024 + hf * 512:
                                            (h % 2) * 1024 + (hf + 1) * 512],
                        kf_st[:, :], waits=[K.ve.dep, K.dep_zero])]

        csv_st = K.row("csv", 512)
        for hg in range(4):
            d_csv = K.dma("scalar", csv_st[0:1, :],
                          K.scr["cco_r2"][0:1, A2_CSV1 + hg * 512:
                                          A2_CSV1 + (hg + 1) * 512],
                          waits=[(K.cc, K.cc_ar2), K.mm.dep],
                          sem_override=K.sem("l0csv"))
            for kc in range(NKC):
                sl, dep_w = wvring.fill(
                    K.scr["cco_pv1"][kc // 2, 0, hg * 4:(hg + 1) * 4,
                                     (kc % 2) * 128:(kc % 2 + 1) * 128, :]
                    .rearrange("h p i -> p h i"),
                    K.scr["cco_pv1"][kc // 2, 1, hg * 4:(hg + 1) * 4,
                                     (kc % 2) * 128:(kc % 2 + 1) * 128, :]
                    .rearrange("h p i -> p h i"),
                    extra_waits=[(K.cc, K.cc_pv1[1])],
                    dslice=lambda a: a.rearrange("p (h i) -> p h i", h=4))
                for sw in range(4):
                    bank = 2 + sw
                    first = (kc == 0)
                    s0 = sw * 128
                    K.E("tensor", lambda e, sl=sl, bank=bank, s0=s0, kc=kc:
                        e.matmul(K.ps[bank][:, :], xs_h[kc][:, s0:s0 + 128],
                                 wvring.hi[sl][:, :], start=(kc == 0),
                                 stop=False),
                        waits=[dep_w, dep_xs] + (K.psw(bank) if first else []))
                    K.E("tensor", lambda e, sl=sl, bank=bank, s0=s0, kc=kc:
                        e.matmul(K.ps[bank][:, :], xs_h[kc][:, s0:s0 + 128],
                                 wvring.lo[sl][:, :], start=False, stop=False))
                    K.E("tensor", lambda e, sl=sl, bank=bank, s0=s0, kc=kc:
                        e.matmul(K.ps[bank][:, :], xs_l[kc][:, s0:s0 + 128],
                                 wvring.hi[sl][:, :], start=False, stop=False))
                wvring.release(sl, K.mm.dep)
            for sw in range(4):
                bank = 2 + sw
                K.E("tensor", lambda e, bank=bank, sw=sw, hf=hf:
                    e.matmul(K.ps[bank][:, :],
                             nm_row[0:1, hf * 512 + sw * 128:
                                    hf * 512 + (sw + 1) * 128],
                             csv_st[0:1, :], start=False, stop=True),
                    waits=[d_csv])
                K.E("vector", lambda e, bank=bank, sw=sw:
                    e.tensor_scalar(kf_st[:, :], K.ps[bank][:, :],
                                    ri_col[sw][:, 0:1], None, ALU.mult),
                    waits=[K.mm.dep, d_ric[-1], K.ve.dep] + kf_rel)
                K.ps_done(bank, K.ve.dep)
                kf_rel = [g_act.store(
                    2, K.scr["cci_kv1"][1024 + hf * 512 + sw * 128:
                                        1024 + hf * 512 + (sw + 1) * 128,
                                        hg * 512:(hg + 1) * 512],
                    kf_st[:, :], waits=[K.ve.dep, K.dep_zero])]
    dep_kv = [(sm, sm.n) for sm in g_act.sems]
    K.cc_kv1 = ag(K, "cci_kv1", "cco_kv1", dep_kv, pairs=True)

    kf = K.vB((128, 2048), F32, 0, "l1kf2")
    vtf = K.vB((128, 2048), F32, 8192, "l1vtf")
    dden = K.vB((128, 512), F32, 16384, "l1dden")
    t2 = K.vB((128, 512), F32, 18432, "l1t2")
    ah = K.vB((128, 512), F32R, 20480, "l1ah2")
    al = K.vB((128, 512), F32R, 22528, "l1al2")
    smsh = K.vB((128, 1), F32, 24576, "l1smsh")
    mtile = [K.vB((128, 512), BF16, 25600, "l1mk0"),
             K.vB((128, 512), BF16, 26624, "l1mk1")]
    kh = at("l1kh", [128, 2048], F32R, offset=K.OFF_A)
    kl = at("l1kl", [128, 2048], F32R, offset=K.OFF_A + 8192)
    vth = at("l1vth", [128, 2048], F32R, offset=K.OFF_A + 16384)
    vtl = at("l1vtl", [128, 2048], F32R, offset=K.OFF_A + 24576)
    qring = PairRing(K, "l0q", (128, 1024), 32768, 2, "sync", region="A")
    qring.rel = [K.phase_gate(), K.phase_gate()]
    p32 = [at("l1p32a", [128, 512], F32, offset=K.OFF_A + 49152),
           at("l1p32b", [128, 512], F32, offset=K.OFF_A + 51200)]
    phh = [at("l1pha", [128, 512], F32R, offset=K.OFF_A + 53248),
           at("l1phb", [128, 512], F32R, offset=K.OFF_A + 55296)]
    pll = [at("l1pla", [128, 512], F32R, offset=K.OFF_A + 57344),
           at("l1plb", [128, 512], F32R, offset=K.OFF_A + 59392)]
    r_mask = Ring(K, "l1msk", mtile, "scalar")
    r_mask.rel = [K.phase_gate(), K.phase_gate()]
    rowst2 = StoreGate(K, "l0row2", 1, "scalar")
    den_row = K.row("r_b", TOK)
    kv_sem = K.sem("l0kvld")
    p_rel = [[], []]
    kv_rel = K.phase_gate()
    K.E("vector", lambda e: e.memset(smsh[:, :], -SM_SHIFT), waits=K.phase_gate())
    dden_rel = []

    for h in range(16):
        d_k = []
        for p in range(2):
            d_k.append(K.dma("sync", kf[:, p * 1024:(p + 1) * 1024],
                             K.scr["cco_kv1"][p, (h // 2) * 128:(h // 2 + 1) * 128,
                                              (h % 2) * 1024:(h % 2 + 1) * 1024],
                             waits=[(K.cc, K.cc_kv1[1])] + kv_rel,
                             sem_override=kv_sem))
            for i in range(8):
                d_k.append(K.dma(
                    "sync", vtf[:, (p * 8 + i) * 128:(p * 8 + i + 1) * 128],
                    K.scr["cco_kv1"][p, 1024 + i * 128:1024 + (i + 1) * 128,
                                     h * 128:(h + 1) * 128],
                    waits=[], sem_override=kv_sem))
        dep_kld = d_k[-1]
        K.E("vector", lambda e: e.tensor_copy(kh[:, :], kf[:, :]),
            waits=[dep_kld, K.ve.dep] + kv_rel)
        K.E("vector", lambda e: e.tensor_sub(
            kl[:, :], kf[:, :], kh.ap().bitcast(F32)[:, :]), waits=[K.ve.dep])
        K.E("vector", lambda e: e.tensor_copy(vth[:, :], vtf[:, :]),
            waits=[K.ve.dep])
        K.E("vector", lambda e: e.tensor_sub(
            vtl[:, :], vtf[:, :], vth.ap().bitcast(F32)[:, :]),
            waits=[K.ve.dep])
        dep_ksp = K.ve.dep
        kv_rel = [dep_ksp]
        sl_q, dep_q = qring.fill(
            K.scr["q1h"][h * 128:(h + 1) * 128, :],
            K.scr["q1l"][h * 128:(h + 1) * 128, :],
            extra_waits=[(g_act.sems[0], g_act.sems[0].n),
                         (g_act.sems[1], g_act.sems[1].n)])
        for qt in range(2):
            qsl = slice(qt * 512, (qt + 1) * 512)
            for sc in range(16):
                sb = sc % 2
                pp = sc % 2
                s0 = sc * 128
                sl_m, t_m, dep_m = r_mask.fill(
                    K.inp["cmask"][qt, sc, :, :])
                K.E("tensor", lambda e, sb=sb, s0=s0, sl_q=sl_q, qsl=qsl:
                    e.matmul(K.ps[sb][:, :], kh[:, s0:s0 + 128],
                             qring.hi[sl_q][:, qsl], start=True, stop=False),
                    waits=[dep_ksp, dep_q] + K.psw(sb))
                K.E("tensor", lambda e, sb=sb, s0=s0, sl_q=sl_q, qsl=qsl:
                    e.matmul(K.ps[sb][:, :], kh[:, s0:s0 + 128],
                             qring.lo[sl_q][:, qsl], start=False, stop=False))
                K.E("tensor", lambda e, sb=sb, s0=s0, sl_q=sl_q, qsl=qsl:
                    e.matmul(K.ps[sb][:, :], kl[:, s0:s0 + 128],
                             qring.hi[sl_q][:, qsl], start=False, stop=True))
                dep_sc = K.mm.dep
                K.E("vector", lambda e, sb=sb, t_m=t_m:
                    e.tensor_add(K.ps[sb][:, :], K.ps[sb][:, :], t_m[:, :]),
                    waits=[dep_sc, dep_m, K.ve.dep])
                r_mask.release(sl_m, K.ve.dep)
                K.E("scalar", lambda e, sb=sb, pp=pp:
                    e.activation(p32[pp][:, :], K.ps[sb][:, :], AF.Exp,
                                 bias=smsh[:, 0:1], scale=ISQ_DH),
                    waits=[K.ve.dep] + p_rel[pp])
                K.ps_done(sb, K.ac.dep)
                K.E("vector", lambda e, pp=pp: e.tensor_copy(
                    phh[pp][:, :], p32[pp][:, :]), waits=[K.ac.dep])
                K.E("vector", lambda e, pp=pp: e.tensor_sub(
                    pll[pp][:, :], p32[pp][:, :],
                    phh[pp].ap().bitcast(F32)[:, :]), waits=[K.ve.dep])
                dep_psp = K.ve.dep
                K.E("tensor", lambda e, pp=pp, sc=sc:
                    e.matmul(K.ps[6][0:1, 0:512], K.ones_r[:, :],
                             phh[pp][:, :], start=(sc == 0), stop=False),
                    waits=[dep_psp] + (K.psw(6) if sc == 0 else []))
                K.E("tensor", lambda e, pp=pp, sc=sc:
                    e.matmul(K.ps[6][0:1, 0:512], K.ones_r[:, :],
                             pll[pp][:, :], start=False, stop=(sc == 15)))
                K.E("tensor", lambda e, pp=pp, sc=sc, s0=s0:
                    e.matmul(K.ps[2][:, :], vth[:, s0:s0 + 128],
                             phh[pp][:, :], start=(sc == 0), stop=False),
                    waits=(K.psw(2) if sc == 0 else []))
                K.E("tensor", lambda e, pp=pp, sc=sc, s0=s0:
                    e.matmul(K.ps[2][:, :], vth[:, s0:s0 + 128],
                             pll[pp][:, :], start=False, stop=False))
                K.E("tensor", lambda e, pp=pp, sc=sc, s0=s0:
                    e.matmul(K.ps[2][:, :], vtl[:, s0:s0 + 128],
                             phh[pp][:, :], start=False, stop=(sc == 15)))
                p_rel[pp] = [K.mm.dep]
            K.E("vector", lambda e: e.tensor_copy(
                den_row[0:1, 0:512], K.ps[6][0:1, 0:512]),
                waits=[K.mm.dep, K.ve.dep])
            K.ps_done(6, K.ve.dep)
            K.E("vector", lambda e: e.reciprocal(
                den_row[0:1, 0:512], den_row[0:1, 0:512]), waits=[K.ve.dep])
            d_dr = rowst2.store(0, K.scr["row_d"][0:1, 0:512],
                                den_row[0:1, 0:512], waits=[K.ve.dep])
            d_db = K.dma("gpsimd", dden[:, :],
                         K.scr["row_d"][0:1, 0:512].broadcast_to([128, 512]),
                         waits=[d_dr] + dden_rel, sem_override=K.sem("l0dden"))
            K.E("vector", lambda e: e.tensor_mul(
                t2[:, :], K.ps[2][:, :], dden[:, :]),
                waits=[K.mm.dep, d_db, K.ve.dep])
            K.ps_done(2, K.ve.dep)
            dden_rel = [K.ve.dep]
            K.E("vector", lambda e: e.tensor_copy(ah[:, :], t2[:, :]),
                waits=[K.ve.dep, (g_act.sems[0], g_act.sems[0].n)])
            K.E("vector", lambda e: e.tensor_sub(
                al[:, :], t2[:, :], ah.ap().bitcast(F32)[:, :]),
                waits=[K.ve.dep, (g_act.sems[1], g_act.sems[1].n)])
            g_act.store(0, K.scr["a1h"][h * 128:(h + 1) * 128, qsl],
                        ah.ap().bitcast(F32)[:, :], waits=[K.ve.dep])
            g_act.store(1, K.scr["a1l"][h * 128:(h + 1) * 128, qsl],
                        al.ap().bitcast(F32)[:, :], waits=[K.ve.dep])
        qring.release(sl_q, K.mm.dep)

    if getattr(K, "skip_l1p3", False):
        return
    a_h = [at(f"l1a3h{kc}", [128, 512], F32R, offset=K.OFF_A + kc * 2048)
           for kc in range(NKC)]
    a_l = [at(f"l1a3l{kc}", [128, 512], F32R, offset=K.OFF_A + 32768 + kc * 2048)
           for kc in range(NKC)]
    woring = PairRing(K, "l0wo", (128, 128), 0, 4, "sync", region="W")
    woring.rel = [K.phase_gate() for _ in range(4)]
    ald_sem = K.sem("l0ald")
    a_gate = [(g_act.sems[0], g_act.sems[0].n), (g_act.sems[1], g_act.sems[1].n)]
    a_rel = K.phase_gate()
    for qt in range(2):
        qsl = slice(qt * 512, (qt + 1) * 512)
        d_a = None
        for kc in range(NKC):
            K.dma("sync", a_h[kc].ap().bitcast(F32)[:, :],
                  K.scr["a1h"][kc * 128:(kc + 1) * 128, qsl],
                  waits=a_gate + a_rel, sem_override=ald_sem)
            d_a = K.dma("sync", a_l[kc].ap().bitcast(F32)[:, :],
                        K.scr["a1l"][kc * 128:(kc + 1) * 128, qsl],
                        waits=[], sem_override=ald_sem)
        for dc in range(NKC):
            bank = dc % 2
            for kc in range(NKC):
                sl, dep_w = woring.fill(
                    K.scr["cco_po1"][kc // 2, 0, (kc % 2) * 128:(kc % 2 + 1) * 128,
                                     dc * 128:(dc + 1) * 128],
                    K.scr["cco_po1"][kc // 2, 1, (kc % 2) * 128:(kc % 2 + 1) * 128,
                                     dc * 128:(dc + 1) * 128],
                    extra_waits=[(K.cc, K.cc_po1[1])])
                first = (kc == 0)
                K.E("tensor", lambda e, sl=sl, kc=kc, bank=bank:
                    e.matmul(K.ps[bank][:, :], woring.hi[sl][:, :],
                             a_h[kc][:, :], start=(kc == 0), stop=False),
                    waits=[dep_w, d_a] + (K.psw(bank) if first else []))
                K.E("tensor", lambda e, sl=sl, kc=kc, bank=bank:
                    e.matmul(K.ps[bank][:, :], woring.hi[sl][:, :],
                             a_l[kc][:, :], start=False, stop=False))
                K.E("tensor", lambda e, sl=sl, kc=kc, bank=bank:
                    e.matmul(K.ps[bank][:, :], woring.lo[sl][:, :],
                             a_h[kc][:, :], start=False, stop=(kc == NKC - 1)))
                woring.release(sl, K.mm.dep)
            K.E("vector", lambda e, bank=bank, dc=dc, qsl=qsl:
                e.tensor_add(K.res[dc][:, qsl], K.res[dc][:, qsl],
                             K.ps[bank][:, :]),
                waits=[K.mm.dep, K.ve.dep])
            K.ps_done(bank, K.ve.dep)
        a_rel = [K.mm.dep]


def ph_l1ffn(K):
    nc = K.nc
    at = nc.alloc_sbuf_tensor_at
    y1 = [at(f"f1y1{kc}", [128, TOK], BF16, offset=K.OFF_A + kc * 2048)
          for kc in range(NKC)]
    y2 = [at(f"f1y2{kc}", [128, TOK], BF16, offset=K.OFF_A + 32768 + kc * 2048)
          for kc in range(NKC)]
    y3 = [at(f"f1y3{ic}", [128, 512], BF16, offset=K.OFF_A + ic * 1024)
          for ic in range(NIC)]
    m1c = [at(f"f1m1{i}", [128, 2048], BF16, offset=K.OFF_W + i * 4096)
           for i in range(2)]
    m2c = [at(f"f1m2{i}", [128, 2048], BF16, offset=K.OFF_W + 8192 + i * 4096)
           for i in range(2)]
    m3c = [at(f"f1m3{i}", [128, 4096], BF16, offset=K.OFF_W + i * 8192)
           for i in range(2)]
    uq = at("f1uq", [128, TOK], F32, offset=K.OFF_W + 0)
    yiq = at("f1yi", [128, TOK], I32, offset=K.OFF_W + 4096)
    tsil = at("f1tsil", [128, 512], F32, offset=K.OFF_B)
    asil = at("f1asil", [128, 512], F32, offset=K.OFF_B + 2048)
    h32 = at("f1h32", [128, 512], F32, offset=K.OFF_B + 4096)
    tv = at("f1tv", [128, 512], F32, offset=K.OFF_B + 6144)
    acc3 = at("f1acc3", [128, TOK], F32, offset=K.OFF_B + 8192)
    x2s = at("f1x2s", [128, 512], F32, offset=K.OFF_B + 12288)
    hin = [at(f"f1hin{i}", [128, 512], F32, offset=K.OFF_B + 14336 + i * 2048)
           for i in range(2)]
    u3 = at("f1u3", [128, 512], F32, offset=K.OFF_B + 18432)
    yi3 = at("f1yi3", [128, 512], I32, offset=K.OFF_B + 20480)
    t3 = at("f1t3", [128, 512], F32, offset=K.OFF_B + 22528)
    rr_row = K.row("r_a", TOK)
    mx_row = K.row("r_b", TOK)
    sq_row = K.row("negmu", TOK)
    c1_row = mx_row
    r_m1 = Ring(K, "f0g", m1c, "sync")
    r_m2 = Ring(K, "f0u", m2c, "sync")
    r_m3 = Ring(K, "f0d", m3c, "sync")
    r_hin = Ring(K, "f0hli", hin, "sync")
    for rng in (r_m1, r_m2, r_m3, r_hin):
        rng.rel = [K.phase_gate() for _ in rng.tiles]
    g_h = StoreGate(K, "f0hlg", 2, "gpsimd")
    rowst = StoreGate(K, "f0row", 1, "scalar")
    bsem = K.sem("f0rt")

    def colsrc(cco, r0, nr, ic):
        return cco[r0:r0 + nr, ic * 128:(ic + 1) * 128].rearrange(
            "(c p) i -> p c i", p=128)
    cslice = lambda t: t.ap().rearrange("p (c i) -> p c i", i=128)

    gate0 = K.phase_gate()
    for half in range(2):
        hs = slice(half * 512, (half + 1) * 512)
        for kc in range(NKC):
            K.E("scalar", lambda e, kc=kc, hs=hs:
                e.activation(x2s[:, :], K.res[kc][:, hs], AF.Square),
                waits=gate0 + [K.mm.dep])
            K.E("tensor", lambda e, kc=kc, half=half:
                e.matmul(K.ps[6 + half][0:1, 0:512], K.ones_f[:, :], x2s[:, :],
                         start=(kc == 0), stop=(kc == NKC - 1)),
                waits=[K.ac.dep] + (K.psw(6 + half) if kc == 0 else []))
    for half in range(2):
        hs = slice(half * 512, (half + 1) * 512)
        K.E("vector", lambda e, hs=hs, half=half: e.tensor_scalar(
            rr_row[0:1, hs], K.ps[6 + half][0:1, 0:512], 1.0 / D, EPS_RMS,
            ALU.mult, ALU.add), waits=[K.mm.dep, K.ve.dep])
        K.ps_done(6 + half, K.ve.dep)
    K.E("scalar", lambda e: e.activation(rr_row[0:1, :], rr_row[0:1, :],
                                         AF.Sqrt), waits=[K.ve.dep])
    K.E("vector", lambda e: e.reciprocal(rr_row[0:1, :], rr_row[0:1, :]),
        waits=[K.ac.dep, K.ve.dep])
    dep_rr = K.ve.dep

    def pmax_tree(acc_t, out_row, extra=()):
        extra = list(extra) + getattr(K, "_dbg_dep", [])
        """max over partitions of acc_t [128, TOK] -> out_row [1, TOK]
        via SBUF->SBUF partition-shift DMAs + abs_max folds."""
        width = 64
        cur = acc_t
        other = K.bc[0]
        dep = K.ve.dep
        while width >= 1:
            d_sh = K.dma("sync", other[0:width, :], cur[width:2 * width, :],
                         waits=[dep] + list(extra),
                         sem_override=K.sem("f0hlin"))
            K.E("vector", lambda e, cur=cur, other=other, width=width:
                e.tensor_max(cur[0:width, :], cur[0:width, :],
                             other[0:width, :]),
                waits=[d_sh, K.ve.dep])
            dep = K.ve.dep
            width //= 2
        K.E("vector", lambda e, cur=cur, out_row=out_row: e.tensor_copy(
            out_row[0:1, :], cur[0:1, :]), waits=[K.ve.dep])

    def quant_rows(tp_idx, dep_mx):
        K.E("vector", lambda e: e.tensor_mul(mx_row[0:1, :], mx_row[0:1, :],
                                             rr_row[0:1, :]),
            waits=[dep_mx, dep_rr, K.ve.dep])
        K.E("vector", lambda e: e.tensor_scalar(
            sq_row[0:1, :], mx_row[0:1, :], 1e-4, None, ALU.max),
            waits=[K.ve.dep])
        K.E("vector", lambda e: e.reciprocal(sq_row[0:1, :], sq_row[0:1, :]),
            waits=[K.ve.dep])
        K.E("vector", lambda e: e.tensor_mul(c1_row[0:1, :], rr_row[0:1, :],
                                             sq_row[0:1, :]), waits=[K.ve.dep])
        K.E("vector", lambda e: e.tensor_scalar(
            c1_row[0:1, :], c1_row[0:1, :], 127.0, None, ALU.mult),
            waits=[K.ve.dep])
        d_q = rowst.store(0, K.scr["row_d"][0:1, :], c1_row[0:1, :],
                          waits=[K.ve.dep] + K.phase_gate())
        d_bq = K.dma("gpsimd", K.bc[0][:, :],
                     K.scr["row_d"][0:1, :].broadcast_to([128, TOK]),
                     waits=[d_q], sem_override=bsem)
        K.E("vector", lambda e: e.reciprocal(c1_row[0:1, :], c1_row[0:1, :]),
            waits=[K.ve.dep, d_q])
        K.E("vector", lambda e: e.tensor_mul(c1_row[0:1, :], c1_row[0:1, :],
                                             rr_row[0:1, :]), waits=[K.ve.dep])
        K.E("vector", lambda e, tp_idx=tp_idx: e.tensor_scalar(
            c1_row[0:1, :], c1_row[0:1, :],
            K.thr_tp[0:1, tp_idx:tp_idx + 1], None, ALU.mult),
            waits=[K.ve.dep, K.dep_thr_tp])
        d_c = rowst.store(0, K.scr["row_d"][0:1, :], c1_row[0:1, :],
                          waits=[K.ve.dep, d_bq])
        d_bc = K.dma("gpsimd", K.bc[1][:, :],
                     K.scr["row_d"][0:1, :].broadcast_to([128, TOK]),
                     waits=[d_c], sem_override=bsem)
        return d_bq, d_bc

    def quant_y(gname, ydst, tp_idx):
        K.E("vector", lambda e: e.memset(acc3[:, :], 0.0),
            waits=[K.ve.dep] + gate0 + getattr(K, "_dbg_dep", []))
        for kc in range(NKC):
            K.E("scalar", lambda e, kc=kc, gname=gname: e.activation(
                uq[:, :], K.res[kc][:, :], AF.Abs,
                scale=K.scl[gname][:, kc:kc + 1]),
                waits=[K.dep_scl, K.ve.dep] + gate0)
            K.E("vector", lambda e: e.tensor_max(
                acc3[:, :], acc3[:, :], uq[:, :]),
                waits=[K.ve.dep, K.ac.dep])
        pmax_tree(acc3, mx_row)
        d_bq, d_bc = quant_rows(tp_idx, K.ve.dep)
        for kc in range(NKC):
            K.E("vector", lambda e, kc=kc, gname=gname:
                e.scalar_tensor_tensor(uq[:, :], K.res[kc][:, :],
                                       K.scl[gname][:, kc:kc + 1], K.bc[0][:, :],
                                       ALU.mult, ALU.mult),
                waits=[d_bq, K.ve.dep] + ([K.mm.dep] if kc == 0 else []))
            K.E("vector", lambda e, kc=kc: e.tensor_copy(yiq[:, :], uq[:, :]),
                waits=[K.ve.dep])
            K.E("vector", lambda e, kc=kc, ydst=ydst: e.tensor_scalar(
                ydst[kc][:, :], yiq[:, :], 127.0, -128.0, ALU.min, ALU.max),
                waits=[K.ve.dep])
        return d_bc

    d_c1 = quant_y("g1", y1, 0)
    if getattr(K, "dbg_l1ffn", False):
        wdep = [K.ve.dep, d_c1]
        t = K.nc.dram_tensor("dbg_y1", [D, TOK], BF16, kind="ExternalOutput").ap()
        for kc in range(NKC):
            K.dma("sync", t[kc * 128:(kc + 1) * 128, :], y1[kc][:, :],
                  waits=wdep, sem_override=K.sem("dbgy"))
        t2d = K.nc.dram_tensor("dbg_dq1", [128, TOK], F32, kind="ExternalOutput").ap()
        K.dma("sync", t2d, K.bc[0][:, :], waits=wdep, sem_override=K.sem("dbgy"))
        t3d = K.nc.dram_tensor("dbg_c1", [128, TOK], F32, kind="ExternalOutput").ap()
        K.dma("sync", t3d, K.bc[1][:, :], waits=wdep, sem_override=K.sem("dbgy"))
        K._dbg_dep = [(K.sem("dbgy"), K.sem("dbgy").n)]
    c1t = at("f1c1t", [128, TOK], F32, offset=K.OFF_B + 24576)
    K.E("vector", lambda e: e.tensor_copy(c1t[:, :], K.bc[1][:, :]),
        waits=[d_c1, K.ve.dep])
    d_c2 = quant_y("g2", y2, 1)
    c2t = at("f1c2t", [128, TOK], F32, offset=K.OFF_B + 28672)
    K.E("vector", lambda e: e.tensor_copy(c2t[:, :], K.bc[1][:, :]),
        waits=[d_c2, K.ve.dep])
    dep_y = K.ve.dep

    K.E("vector", lambda e: e.memset(acc3[:, :], 0.0), waits=[K.ve.dep])
    hst_rel = [[], []]
    for ic in range(NIC):
        s1, t_m1, dep_m1 = r_m1.fill(colsrc(K.scr["cco_m1"], 0, D, ic),
                                     tile_slice=cslice,
                                     extra_waits=[(K.cc, K.cc_m1[1])] +
                                     (K.phase_gate() if ic == 0 else []))
        s2, t_m2, dep_m2 = r_m2.fill(colsrc(K.scr["cco_m2"], 0, D, ic),
                                     tile_slice=cslice,
                                     extra_waits=[(K.cc, K.cc_m2[1])])
        for half in range(2):
            hs = slice(half * 512, (half + 1) * 512)
            for kc in range(NKC):
                K.E("tensor", lambda e, t_m1=t_m1, kc=kc, half=half, hs=hs:
                    e.matmul(K.ps[half][:, :], t_m1[:, kc * 128:(kc + 1) * 128],
                             y1[kc][:, hs], start=(kc == 0),
                             stop=(kc == NKC - 1)),
                    waits=[dep_m1, dep_y] + (K.psw(half) if kc == 0 else []))
                K.E("tensor", lambda e, t_m2=t_m2, kc=kc, half=half, hs=hs:
                    e.matmul(K.ps[2 + half][:, :],
                             t_m2[:, kc * 128:(kc + 1) * 128],
                             y2[kc][:, hs], start=(kc == 0),
                             stop=(kc == NKC - 1)),
                    waits=[dep_m2] + (K.psw(2 + half) if kc == 0 else []))
        r_m1.release(s1, K.mm.dep)
        r_m2.release(s2, K.mm.dep)
        for half in range(2):
            hs = slice(half * 512, (half + 1) * 512)
            K.E("vector", lambda e, half=half, hs=hs: e.tensor_mul(
                tsil[:, :], K.ps[half][:, :], c1t[:, hs]),
                waits=[K.mm.dep, K.ve.dep, K.ac.dep])
            K.ps_done(half, K.ve.dep)
            K.E("scalar", lambda e: e.activation(asil[:, :], tsil[:, :],
                                                 AF.Silu),
                waits=[K.ve.dep])
            K.E("vector", lambda e, half=half, hs=hs: e.tensor_mul(
                tv[:, :], K.ps[2 + half][:, :], c2t[:, hs]),
                waits=[K.mm.dep, K.ve.dep])
            K.ps_done(2 + half, K.ve.dep)
            K.E("vector", lambda e: e.tensor_mul(h32[:, :], asil[:, :],
                                                 tv[:, :]),
                waits=[K.ve.dep, K.ac.dep] + hst_rel[0] + hst_rel[1])
            K.E("scalar", lambda e, ic=ic: e.activation(
                tsil[:, :], h32[:, :], AF.Abs,
                scale=K.scl["g3"][:, ic:ic + 1]),
                waits=[K.ve.dep, K.dep_scl])
            K.E("vector", lambda e, ic=ic, hs=hs: e.tensor_max(
                acc3[:, hs], acc3[:, hs], tsil[:, :]),
                waits=[K.ve.dep, K.ac.dep])
            K.E("scalar", lambda e: e.activation(x2s[:, :], h32[:, :],
                                                 AF.Square),
                waits=[K.ve.dep, K.mm.dep])
            K.E("tensor", lambda e, ic=ic, half=half:
                e.matmul(K.ps[6 + half][0:1, 0:512], K.ones_f[:, :],
                         x2s[:, :], start=(ic == 0), stop=(ic == NIC - 1)),
                waits=[K.ac.dep] + (K.psw(6 + half) if ic == 0 else []))
            hst_rel[half] = [g_h.store(
                half, K.scr["hbuf1"][ic * 128:(ic + 1) * 128, hs], h32[:, :],
                waits=[K.ve.dep] + hst_rel[half])]

    for half in range(2):
        hs = slice(half * 512, (half + 1) * 512)
        K.E("vector", lambda e, hs=hs, half=half: e.tensor_scalar(
            rr_row[0:1, hs], K.ps[6 + half][0:1, 0:512], 1.0 / II, EPS_RMS,
            ALU.mult, ALU.add), waits=[K.mm.dep, K.ve.dep])
        K.ps_done(6 + half, K.ve.dep)
    K.E("scalar", lambda e: e.activation(rr_row[0:1, :], rr_row[0:1, :],
                                         AF.Sqrt), waits=[K.ve.dep])
    K.E("vector", lambda e: e.reciprocal(rr_row[0:1, :], rr_row[0:1, :]),
        waits=[K.ac.dep, K.ve.dep])
    dep_rr = K.ve.dep
    pmax_tree(acc3, mx_row, extra=[K.mm.dep])
    d_bq3, d_bc3 = quant_rows(2, K.ve.dep)
    c3t = c1t
    K.E("vector", lambda e: e.tensor_copy(c3t[:, :], K.bc[1][:, :]),
        waits=[d_bc3, K.ve.dep])

    for half in range(2):
        hs = slice(half * 512, (half + 1) * 512)
        y3_gate = K.phase_gate() if half == 0 else [K.ve.dep]
        for ic in range(NIC):
            sl_h, t_h, dep_h = r_hin.fill(
                K.scr["hbuf1"][ic * 128:(ic + 1) * 128, hs],
                extra_waits=[(g_h.sems[0], g_h.sems[0].n),
                             (g_h.sems[1], g_h.sems[1].n)] if ic < 2 else [])
            K.E("vector", lambda e, t_h=t_h, ic=ic, hs=hs:
                e.scalar_tensor_tensor(u3[:, :], t_h[:, :],
                                       K.scl["g3"][:, ic:ic + 1],
                                       K.bc[0][:, hs], ALU.mult, ALU.mult),
                waits=[dep_h, d_bq3, K.ve.dep] + (y3_gate if ic == 0 else []))
            r_hin.release(sl_h, K.ve.dep)
            K.E("vector", lambda e: e.tensor_copy(yi3[:, :], u3[:, :]),
                waits=[K.ve.dep])
            K.E("vector", lambda e, ic=ic: e.tensor_scalar(
                y3[ic][:, :], yi3[:, :], 127.0, -128.0, ALU.min, ALU.max),
                waits=[K.ve.dep])
        dep_y3 = K.ve.dep
        for dc in range(NKC):
            bank = 4 + dc % 2
            dhalf = []
            for i2 in range(2):
                sl_d, td, dep_d = r_m3.fill(
                    K.scr["cco_m3"][i2 * 4096:(i2 + 1) * 4096,
                                    dc * 128:(dc + 1) * 128].rearrange(
                        "(c p) i -> p c i", p=128),
                    tile_slice=lambda t: t.ap().rearrange(
                        "p (c i) -> p c i", i=128),
                    extra_waits=[(K.cc, K.cc_m3[1])] +
                                (K.phase_gate() if (half == 0 and dc == 0)
                                 else []))
                dhalf.append((sl_d, td, dep_d))
            for ic in range(NIC):
                sl_d, td, dep_d = dhalf[ic // 32]
                i32 = ic % 32
                K.E("tensor", lambda e, td=td, i32=i32, ic=ic, bank=bank:
                    e.matmul(K.ps[bank][:, :], td[:, i32 * 128:(i32 + 1) * 128],
                             y3[ic][:, :], start=(ic == 0),
                             stop=(ic == NIC - 1)),
                    waits=[dep_d, dep_y3] + (K.psw(bank) if ic == 0 else []))
            for i2 in range(2):
                r_m3.release(dhalf[i2][0], K.mm.dep)
            K.E("vector", lambda e, bank=bank, hs=hs: e.tensor_mul(
                t3[:, :], K.ps[bank][:, :], c3t[:, hs]),
                waits=[K.mm.dep, K.ve.dep])
            K.ps_done(bank, K.ve.dep)
            K.E("vector", lambda e, dc=dc, hs=hs: e.tensor_add(
                K.res[dc][:, hs], K.res[dc][:, hs], t3[:, :]),
                waits=[K.ve.dep])
